# version 11
# baseline (speedup 1.0000x reference)
import contextlib
import math
import numpy as np
import concourse.bass as bass
import concourse.mybir as mybir

F32 = mybir.dt.float32
BF16 = mybir.dt.bfloat16
I8 = mybir.dt.int8
AF = mybir.ActivationFunctionType
ALU = mybir.AluOpType
AX = mybir.AxisListType

C = 2048
T = 2048
H = 16
D = 128
HID = 8192
NCH = C // 128
NTOK = T // 128
SCALE = D ** -0.5
NEG = -1.0e9

CON_EPS = 0
CON_SCALE = 1
NCONST = 2

ENGS = ["sync", "tensor", "vector", "scalar", "gpsimd"]


class P:
    def __init__(self, active, eng, sems):
        self.a = active
        self.e = eng
        self.sems = sems
        self.cnt = {k: 0 for k in sems}

    def on(self, name):
        return self.a == name

    def inc(self, sem, instr, n=1):
        self.cnt[sem] += n
        if instr is not None:
            instr.then_inc(self.sems[sem], n)
        return instr

    def wait(self, eng, sem, val=None):
        v = self.cnt[sem] if val is None else val
        if self.a == eng and v > 0:
            self.e.wait_ge(self.sems[sem], v)


SEMS = [
    "pre", "ldx0", "ldx1", "ldw0", "ldw1", "xsq", "rsq", "var", "chunk", "xqbf",
    "tpose", "evac", "wabs", "wsum", "mmini", "r1T", "gb",
    "ldh", "terk", "qkvg", "qkve", "vtp", "vte", "nrm2", "nrme", "negme",
    "st_", "ste", "exp", "av", "ave", "ost",
    "ostd0", "ostd1", "ostd2", "ostd3", "ldp0", "ldp1", "pabs",
    "ldo0", "ldo1", "ldo2", "ldq0", "ldq1", "ldq2", "ternp", "oq", "oqT", "oqTe", "projg", "proje",
    "ldx3", "x2st", "vs3", "sv3",
    "l3", "l3tp", "l3te", "ldwg", "ldwu", "wsp", "gu", "sil", "gsp",
    "ldwd0", "ldwd1", "wdsp", "dn", "dne", "accw0", "accw1", "accw2",
    "ldac0", "ldac1",
    "outw0", "outw1", "vs4", "sv4",
    "dbgs",
]


def build(debug=(), phases=("p0", "p12", "p3", "p4")):
    nc = bass.Bass("TRN2", target_bir_lowering=False)

    x_d = nc.dram_tensor("x", [T, C], F32, kind="ExternalInput")
    wqkvT_d = nc.dram_tensor("wqkvT", [NCH, 128, 3 * C], F32, kind="ExternalInput")
    wprojT_d = nc.dram_tensor("wprojT", [NCH, 128, C], F32, kind="ExternalInput")
    wgT_d = nc.dram_tensor("wgT", [HID // 128, C, 128], F32, kind="ExternalInput")
    wuT_d = nc.dram_tensor("wuT", [HID // 128, C, 128], F32, kind="ExternalInput")
    wdT_d = nc.dram_tensor("wdT", [HID, C], F32, kind="ExternalInput")
    identf_d = nc.dram_tensor("identf", [128, 128], F32, kind="ExternalInput")
    identb_d = nc.dram_tensor("identb", [128, 128], BF16, kind="ExternalInput")
    masks_d = nc.dram_tensor("masks", [128, 4 * 512], BF16, kind="ExternalInput")
    consts_d = nc.dram_tensor("consts", [128, NCONST], F32, kind="ExternalInput")
    ones_d = nc.dram_tensor("ones", [128, 2], F32, kind="ExternalInput")

    out_d = nc.dram_tensor("out", [T, C], F32, kind="ExternalOutput")
    o_hbm = nc.dram_tensor("o_scr", [H, T, D], F32,
                           kind="ExternalOutput" if "O" in debug else "Internal")
    x2_hbm = nc.dram_tensor("x2_scr", [T, C], F32,
                            kind="ExternalOutput" if "x2" in debug else "Internal")
    acc_hbm = nc.dram_tensor("acc_scr", [T, C], F32, kind="Internal")
    r1_hbm = nc.dram_tensor("r1_scr", [NTOK, 128], F32, kind="Internal")

    dbg = {}

    def dbg_out(name, shape, dtype=F32):
        if name in debug:
            dbg[name] = nc.dram_tensor("dbg_" + name, shape, dtype, kind="ExternalOutput")
        return dbg.get(name)

    dbg_out("xqT", [128, NCH * T], BF16)
    dbg_out("r1", [128, NTOK])
    dbg_out("wmean", [128, 2])
    dbg_out("R1rep", [128, T])
    dbg_out("QK", [128, 2 * T])
    dbg_out("Vaug", [128, NTOK * 129])
    dbg_out("TERN", [128, 3 * C], BF16)

    with contextlib.ExitStack() as ctx:
        E = ctx.enter_context
        sems = {n: E(nc.semaphore(n)) for n in SEMS}

        A1 = E(nc.sbuf_tensor("sb_A1", [128, 32768], BF16))
        A2 = E(nc.sbuf_tensor("sb_A2", [128, 32768], BF16))
        WSLAB = E(nc.sbuf_tensor("sb_WSLAB", [128, 6144], F32))
        XT = [E(nc.sbuf_tensor(f"sb_XT{i}", [128, C], F32)) for i in range(2)]
        TMP = E(nc.sbuf_tensor("sb_TMP", [128, C], F32))
        XQBF = E(nc.sbuf_tensor("sb_XQBF", [128, C], BF16))
        GARB = E(nc.sbuf_tensor("sb_GARB", [128, 1024], F32))
        QF16 = E(nc.sbuf_tensor("sb_QF16", [128, C], mybir.dt.float16))
        MASKSB = E(nc.sbuf_tensor("sb_MASKSB", [128, 4 * 512], BF16))
        R1REP = E(nc.sbuf_tensor("sb_R1REP", [128, T], F32))
        SCR = E(nc.sbuf_tensor("sb_SCR", [128, 1024], F32))
        identf = E(nc.sbuf_tensor("sb_identf", [128, 128], F32))
        identb = E(nc.sbuf_tensor("sb_identb", [128, 128], BF16))
        consts = E(nc.sbuf_tensor("sb_consts", [128, NCONST], F32))
        onesf = E(nc.sbuf_tensor("sb_onesf", [128, 2], F32))
        onesb = E(nc.sbuf_tensor("sb_onesb", [128, 1], BF16))
        ones1 = E(nc.sbuf_tensor("sb_ones1", [1, 128], F32))
        stats = E(nc.sbuf_tensor("sb_stats", [128, 160], F32))
        wabs = E(nc.sbuf_tensor("sb_wabs", [128, 96], F32))
        wabs2 = E(nc.sbuf_tensor("sb_wabs2", [128, 32], F32))
        r1c = E(nc.sbuf_tensor("sb_r1c", [128, NTOK], F32))
        r1T = E(nc.sbuf_tensor("sb_r1T", [NTOK, 128], F32))

        sc1 = E(nc.sbuf_tensor("sb_sc1", [1, 32], F32))
        bc = E(nc.sbuf_tensor("sb_bc", [128, 12], F32))
        r1row = R1REP[0:1, :]
        BC_MEANQ, BC_SWQ, BC_MEANP, BC_SWP, BC_NEGM, BC_THP, BC_THN, BC_THP2, BC_THN2 = 0, 1, 2, 3, 4, 5, 6, 7, 8

        pb = [E(nc.psum_tensor(f"ps_pb{i}", [128, 512], F32)) for i in range(8)]

        def f32v(t, c0, n):
            return t[:, c0:c0 + 2 * n].bitcast(F32)

        WB = [f32v(A2, 0, 6144), f32v(A2, 12288, 6144)]
        Qh = f32v(A2, 0, T)
        Kh = f32v(A2, 4096, T)
        Vaug = f32v(A2, 8192, NTOK * 129)
        ERING = [f32v(A2, 12320 + 1024 * i, 512) for i in range(8)]
        TERN = A2[:, 20512:26656]
        OST = [f32v(A2, 26656 + 256 * i, 128) for i in range(4)]
        Q2B = A2[:, 28672:30720]
        K2B = A2[:, 30720:32768]
        AO3 = f32v(A2, 0, C)
        X2S = f32v(A2, 4096, C)

        S_XSUM, S_XSQ, S_NEGM, S_TV, S_TM2, S_VAR, S_R = 0, 16, 32, 48, 64, 80, 96
        S_SD = 112
        S_MX, S_T, S_T2, S_SR, S_C1, S_CC, S_RS = 144, 145, 146, 147, 148, 149, 150

        block = E(nc.Block())

        def p0(p: P):
            y = p.e if p.on("sync") else None
            g = p.e if p.on("gpsimd") else None
            t = p.e if p.on("tensor") else None
            v = p.e if p.on("vector") else None
            s = p.e if p.on("scalar") else None

            if s:
                s.preload_act_table(AF.Sqrt)

            for ap_sb, ap_d in ((identf, identf_d), (identb, identb_d), (MASKSB, masks_d),
                                (consts, consts_d), (onesf, ones_d)):
                p.inc("pre", y.dma_start(out=ap_sb[:], in_=ap_d[:]) if y else None, 16)

            for i in range(NTOK):
                if y and i >= 2:
                    y.wait_ge(sems["chunk"], i - 1)
                p.inc(f"ldx{i % 2}", y.dma_start(out=XT[i % 2][:], in_=x_d[128 * i:128 * (i + 1), :])
                      if y else None, 16)

            for j in range(NCH):
                if g and j >= 2:
                    g.wait_ge(sems["wabs"], 6 * (j - 1))
                p.inc(f"ldw{j % 2}", g.dma_start(out=WB[j % 2], in_=wqkvT_d[j, :, :]) if g else None, 16)

            p.wait("vector", "pre")
            if v:
                v.tensor_copy(onesb[:], onesf[:, 0:1])
                v.memset(ones1[:], 1.0)

            for i in range(NTOK):
                p.wait("scalar", f"ldx{i % 2}", 16 * (i // 2 + 1))
                ins = None
                if s:
                    s.activation(GARB[:, 0:1024], XT[i % 2][:, 0:1024], AF.Square,
                                 accum_out=stats[:, S_XSQ + i:S_XSQ + i + 1])
                    s.drain()
                    ins = s.activation(GARB[:, 0:1024], XT[i % 2][:, 1024:2048], AF.Square,
                                       accum_out=stats[:, S_SD + i:S_SD + i + 1])
                    s.drain()
                p.inc("xsq", ins)
                p.wait("scalar", "var", i + 1)
                p.inc("rsq", s.activation(stats[:, S_SD + i:S_SD + i + 1],
                                          stats[:, S_VAR + i:S_VAR + i + 1], AF.Sqrt) if s else None)
            for j in range(NCH):
                p.wait("scalar", f"ldw{j % 2}", 16 * (j // 2 + 1))
                for m in range(6):
                    col = 6 * j + m
                    p.inc("wabs", s.activation(GARB[:, 0:1024],
                                               WB[j % 2][:, 1024 * m:1024 * (m + 1)],
                                               AF.Abs, accum_out=wabs[:, col:col + 1])
                          if s else None)
                    if s:
                        s.drain()

            for i in range(NTOK):
                p.wait("vector", f"ldx{i % 2}", 16 * (i // 2 + 1))
                if v:
                    v.reduce_sum(stats[:, S_XSUM + i:S_XSUM + i + 1], XT[i % 2][:], axis=AX.X)
                    v.drain()
                    v.tensor_scalar(stats[:, S_NEGM + i:S_NEGM + i + 1],
                                    stats[:, S_XSUM + i:S_XSUM + i + 1], -1.0 / C, 0.0, ALU.mult, ALU.add)
                p.wait("vector", "xsq", i + 1)
                if v:
                    v.tensor_add(stats[:, S_TV + i:S_TV + i + 1],
                                 stats[:, S_XSQ + i:S_XSQ + i + 1],
                                 stats[:, S_SD + i:S_SD + i + 1])
                    v.drain()
                    v.tensor_scalar(stats[:, S_TV + i:S_TV + i + 1],
                                    stats[:, S_TV + i:S_TV + i + 1], 1.0 / C, 1e-5, ALU.mult, ALU.add)
                    v.drain()
                    v.tensor_mul(stats[:, S_TM2 + i:S_TM2 + i + 1],
                                 stats[:, S_NEGM + i:S_NEGM + i + 1],
                                 stats[:, S_NEGM + i:S_NEGM + i + 1])
                    v.drain()
                p.inc("var", v.tensor_sub(stats[:, S_VAR + i:S_VAR + i + 1],
                                          stats[:, S_TV + i:S_TV + i + 1],
                                          stats[:, S_TM2 + i:S_TM2 + i + 1]) if v else None)
                if v:
                    v.tensor_scalar(TMP[:], XT[i % 2][:],
                                    stats[:, S_NEGM + i:S_NEGM + i + 1], 0.0, ALU.add, ALU.add)
                    v.drain()
                p.inc("chunk", v.reduce_max(stats[:, S_MX:S_MX + 1], TMP[:], axis=AX.X,
                                            apply_absolute_value=True) if v else None)
                p.wait("vector", "rsq", i + 1)
                if v:
                    v.reciprocal(stats[:, S_R + i:S_R + i + 1], stats[:, S_SD + i:S_SD + i + 1])
                    v.drain()
                    v.tensor_mul(stats[:, S_T:S_T + 1], stats[:, S_MX:S_MX + 1],
                                 stats[:, S_R + i:S_R + i + 1])
                    v.drain()
                    v.tensor_scalar(stats[:, S_T2:S_T2 + 1], stats[:, S_T:S_T + 1],
                                    1e-5, 0.0, ALU.max, ALU.add)
                    v.drain()
                    v.reciprocal(stats[:, S_SR:S_SR + 1], stats[:, S_T2:S_T2 + 1])
                    v.drain()
                    v.tensor_mul(stats[:, S_C1:S_C1 + 1], stats[:, S_SR:S_SR + 1],
                                 stats[:, S_R + i:S_R + i + 1])
                    v.tensor_scalar(r1c[:, i:i + 1], stats[:, S_T2:S_T2 + 1],
                                    1.0 / 127.0, 0.0, ALU.mult, ALU.add)
                    v.drain()
                    v.tensor_scalar(stats[:, S_CC:S_CC + 1], stats[:, S_C1:S_C1 + 1],
                                    127.0, 0.0, ALU.mult, ALU.add)
                    v.drain()
                    v.tensor_scalar(TMP[:], TMP[:], stats[:, S_CC:S_CC + 1],
                                    127.49999237060547, ALU.mult, ALU.min)
                    v.drain()
                    v.tensor_scalar(QF16[:], TMP[:], -128.5, 1536.0, ALU.max, ALU.add)
                    v.drain()
                p.inc("xqbf", v.tensor_scalar(XQBF[:], QF16[:], -1536.0, 0.0,
                                              ALU.add, ALU.add) if v else None)

                for grp in range(4):
                    gidx = 4 * i + grp
                    p.wait("tensor", "xqbf", i + 1)
                    if gidx >= 2:
                        p.wait("tensor", "evac", gidx - 1)
                    ins = None
                    if t:
                        for jj in range(4):
                            j = grp * 4 + jj
                            ins = t.transpose(pb[gidx % 2][:, 64 * jj:64 * (jj + 1)].bitcast(BF16),
                                              XQBF[:, 128 * j:128 * (j + 1)], identb[:])
                    p.inc("tpose", ins)
                    p.wait("vector", "tpose", gidx + 1)
                    ins = None
                    if v:
                        for jj in range(4):
                            j = grp * 4 + jj
                            ins = v.tensor_copy(A1[:, T * j + 128 * i: T * j + 128 * (i + 1)],
                                                pb[gidx % 2][:, 64 * jj:64 * (jj + 1)].bitcast(BF16))
                    p.inc("evac", ins)

            p.wait("tensor", "xqbf", NTOK)
            p.wait("tensor", "evac", 4 * NTOK)
            p.inc("tpose", t.transpose(pb[2][0:NTOK, 0:128], r1c[:], identf[:]) if t else None)
            p.wait("vector", "tpose", p.cnt["tpose"])
            p.inc("r1T", v.tensor_copy(r1T[:], pb[2][0:NTOK, 0:128]) if v else None)
            p.wait("gpsimd", "r1T")
            p.inc("gb", g.dma_start(out=r1_hbm[:], in_=r1T[:]) if g else None, 16)
            p.wait("gpsimd", "gb")
            p.inc("gb", g.dma_start(out=r1row[:], in_=bass.AP(r1_hbm, 0, [[T, 1], [1, T]]))
                  if g else None, 16)

            p.wait("vector", "wabs", 6 * NCH)
            p.inc("wsum", v.reduce_sum(stats[:, S_MX:S_MX + 1], wabs[:], axis=AX.X) if v else None)
            p.wait("tensor", "wsum")
            p.inc("mmini", t.matmul(pb[3][0:1, 0:1], stats[:, S_MX:S_MX + 1], onesf[:, 0:1],
                                    start=True, stop=True) if t else None)
            p.wait("vector", "mmini")
            if v:
                v.tensor_scalar(sc1[0:1, 0:1], pb[3][0:1, 0:1], 1.0 / (3 * C * C), 0.0, ALU.mult, ALU.add)
                v.drain()
                v.tensor_scalar(sc1[0:1, 1:2], sc1[0:1, 0:1], 1e-5, 0.0, ALU.max, ALU.add)
                v.drain()
            p.inc("wsum", v.reciprocal(sc1[0:1, 2:3], sc1[0:1, 1:2]) if v else None)
            p.wait("tensor", "wsum")
            ins = None
            if t:
                t.matmul(pb[3][:, 0:1], ones1[:], sc1[0:1, 1:2], start=True, stop=True)
                ins = t.matmul(pb[3][:, 4:5], ones1[:], sc1[0:1, 2:3], start=True, stop=True)
            p.inc("mmini", ins)
            p.wait("vector", "mmini")
            ins = None
            if v:
                v.tensor_copy(bc[:, BC_MEANQ:BC_MEANQ + 1], pb[3][:, 0:1])
                v.tensor_copy(bc[:, BC_SWQ:BC_SWQ + 1], pb[3][:, 4:5])
                v.drain()
                v.tensor_scalar(bc[:, BC_THP:BC_THP + 1], bc[:, BC_MEANQ:BC_MEANQ + 1],
                                0.5, 0.0, ALU.mult, ALU.add)
                ins = v.tensor_scalar(bc[:, BC_THN:BC_THN + 1], bc[:, BC_MEANQ:BC_MEANQ + 1],
                                      -0.5, 0.0, ALU.mult, ALU.add)
            p.inc("evac", ins)

            p.wait("vector", "gb")
            p.wait("vector", "evac")
            p.inc("r1T", v.tensor_scalar(r1row[:], r1row[:],
                                         bc[0:1, BC_MEANQ:BC_MEANQ + 1],
                                         0.0, ALU.mult, ALU.add) if v else None)
            p.wait("tensor", "r1T")
            ins = None
            if t:
                for k in range(4):
                    ins = t.matmul(pb[4 + k][:, 0:512], ones1[:],
                                   r1row[0:1, 512 * k:512 * (k + 1)], start=True, stop=True)
            p.inc("mmini", ins)
            p.wait("vector", "mmini")
            ins = None
            if v:
                for k in range(4):
                    ins = v.tensor_copy(R1REP[:, 512 * k:512 * (k + 1)], pb[4 + k][:, 0:512])
            p.inc("evac", ins)

        def p12(p: P):
            y = p.e if p.on("sync") else None
            g = p.e if p.on("gpsimd") else None
            t = p.e if p.on("tensor") else None
            v = p.e if p.on("vector") else None
            s = p.e if p.on("scalar") else None

            if s:
                s.preload_act_table(AF.Exp)

            ntile = 0
            exp_after = {}
            ste_after = {}
            ndiag = 0
            ngq = 0
            nost = 0

            def stream_head_w(h):
                for j in range(NCH):
                    if y:
                        if h >= 1 and j == 0:
                            y.wait_ge(sems["terk"], h)
                        src = bass.AP(wqkvT_d, j * 128 * (3 * C) + 128 * h,
                                      [[3 * C, 128], [C, 3], [1, 128]])
                        dst = bass.AP(WSLAB, 128 * j, [[6144, 128], [C, 3], [1, 128]])
                        ins = y.dma_start(out=dst, in_=src)
                    else:
                        ins = None
                    p.inc("ldh", ins, 16)

            def ternarize_head(h):
                p.wait("vector", "ldh", 16 * NCH * (h + 1))
                if h >= 1:
                    p.wait("vector", "qkvg", 12 * h)
                ins = None
                if v:
                    for m in range(3):
                        v.tensor_scalar(XQBF[:], WSLAB[:, 2048 * m:2048 * (m + 1)],
                                        bc[:, BC_THN:BC_THN + 1], 0.0, ALU.is_lt, ALU.add)
                        v.drain()
                        ins = v.scalar_tensor_tensor(TERN[:, 2048 * m:2048 * (m + 1)],
                                                     WSLAB[:, 2048 * m:2048 * (m + 1)],
                                                     bc[:, BC_THP:BC_THP + 1], XQBF[:],
                                                     ALU.is_gt, ALU.subtract)
                        v.drain()
                p.inc("terk", ins)

            stream_head_w(0)
            ternarize_head(0)
            for h in range(H):
                if h + 1 < H:
                    stream_head_w(h + 1)

                if h < NCH:
                    if g and h >= 2:
                        g.wait_ge(sems["pabs"], 2 * (h - 1))
                    p.inc(f"ldp{h % 2}", g.dma_start(out=XT[h % 2][:], in_=wprojT_d[h, :, :])
                          if g else None, 16)
                    p.wait("scalar", f"ldp{h % 2}", 16 * (h // 2 + 1))
                    for m in range(2):
                        p.inc("pabs", s.activation(GARB[:, 0:1024],
                                                   XT[h % 2][:, 1024 * m:1024 * (m + 1)], AF.Abs,
                                                   accum_out=wabs2[:, 2 * h + m:2 * h + m + 1])
                              if s else None)
                        if s:
                            s.drain()

                p.wait("tensor", "terk", h + 1)
                dest = {0: Qh, 1: Kh}
                for m in range(2):
                    for gq in range(4):
                        grp = 12 * h + m * 4 + gq
                        bank = pb[grp % 2]
                        if m == 0 and gq == 0:
                            p.wait("tensor", "ave", max(0, 16 * h - 1))
                        if grp >= 2:
                            p.wait("tensor", "qkve", grp - 1)
                        ins = None
                        if t:
                            for j in range(NCH):
                                ins = t.matmul(bank[:, 0:512],
                                               TERN[:, 2048 * m + 128 * j:2048 * m + 128 * (j + 1)],
                                               A1[:, T * j + 512 * gq: T * j + 512 * (gq + 1)],
                                               start=(j == 0), stop=(j == NCH - 1))
                        p.inc("qkvg", ins)
                        p.wait("vector", "qkvg", grp + 1)
                        ins = None
                        if v:
                            dst = dest[m][:, 512 * gq:512 * (gq + 1)] if m < 2 else \
                                TMP[:, 512 * gq:512 * (gq + 1)]
                            ins = v.tensor_mul(dst, bank[:, 0:512], R1REP[:, 512 * gq:512 * (gq + 1)])
                        p.inc("qkve", ins)

                p.wait("scalar", "qkve", 12 * h + 8)
                sq = s.activation(Q2B[:], Qh, AF.Square) if s else None
                p.inc("exp", sq)
                sq = s.activation(K2B[:], Kh, AF.Square) if s else None
                p.inc("exp", sq)
                exp_after["sq"] = p.cnt["exp"]
                p.wait("tensor", "exp", exp_after["sq"])
                p.wait("tensor", "ave", 16 * h)
                p.wait("tensor", "negme", h)
                p.wait("tensor", "vte", 5 * h)
                for k in range(8):
                    srcb = Q2B if k < 4 else K2B
                    p.inc("nrm2", t.matmul(pb[k][0:1, 0:512], onesb[:],
                                           srcb[:, 512 * (k % 4):512 * (k % 4 + 1)],
                                           start=True, stop=True) if t else None)
                for k in range(8):
                    p.wait("vector", "nrm2", 9 * h + k + 1)
                    p.inc("nrme", v.reduce_max(sc1[0:1, 8 + k:9 + k],
                                               pb[k][0:1, 0:512], axis=AX.X) if v else None)
                p.wait("vector", "nrme", 9 * h + 8)
                if v:
                    v.reduce_max(sc1[0:1, 16:17], sc1[0:1, 8:12], axis=AX.X)
                    v.reduce_max(sc1[0:1, 17:18], sc1[0:1, 12:16], axis=AX.X)
                    v.drain()
                    v.tensor_add(sc1[0:1, 18:19], sc1[0:1, 16:17], sc1[0:1, 17:18])
                    v.drain()
                p.inc("nrme", v.tensor_scalar(sc1[0:1, 19:20], sc1[0:1, 18:19],
                                              -0.5 * SCALE, 0.0, ALU.mult, ALU.add) if v else None)
                ins = None
                if v:
                    for tt in range(NTOK):
                        ins = v.memset(Vaug[:, 129 * tt + 128:129 * tt + 129], 1.0)
                p.inc("vte", ins)

                for gq in range(4):
                    grp = 12 * h + 8 + gq
                    bank = pb[grp % 2]
                    if gq < 2:
                        p.wait("tensor", "nrme", 9 * h + gq + 1)
                    p.wait("tensor", "qkve", grp - 1)
                    ins = None
                    if t:
                        for j in range(NCH):
                            ins = t.matmul(bank[:, 0:512],
                                           TERN[:, 4096 + 128 * j:4096 + 128 * (j + 1)],
                                           A1[:, T * j + 512 * gq: T * j + 512 * (gq + 1)],
                                           start=(j == 0), stop=(j == NCH - 1))
                    p.inc("qkvg", ins)
                    p.wait("vector", "qkvg", grp + 1)
                    p.inc("qkve", v.tensor_mul(TMP[:, 512 * gq:512 * (gq + 1)], bank[:, 0:512],
                                               R1REP[:, 512 * gq:512 * (gq + 1)]) if v else None)

                p.wait("tensor", "nrme", p.cnt["nrme"])
                p.inc("nrm2", t.matmul(pb[7][:, 256:257], ones1[:], sc1[0:1, 19:20],
                                       start=True, stop=True) if t else None)
                p.wait("vector", "nrm2", p.cnt["nrm2"])
                p.inc("negme", v.tensor_copy(bc[:, BC_NEGM:BC_NEGM + 1], pb[7][:, 256:257])
                      if v else None)

                for grp in range(4):
                    p.wait("tensor", "qkve", 12 * h + 9 + grp)
                    if grp == 0:
                        p.wait("tensor", "nrme", 9 * h + 3)
                    if grp >= 1:
                        p.wait("tensor", "vte", 5 * h + 1 + grp)
                    ins = None
                    if t:
                        for jj in range(4):
                            tt = grp * 4 + jj
                            ins = t.transpose(pb[2][:, 128 * jj:128 * (jj + 1)],
                                              TMP[:, 128 * tt:128 * (tt + 1)], identf[:])
                            ins.annotate(f"vtp h={h} grp={grp} jj={jj}")
                    p.inc("vtp", ins)
                    p.wait("vector", "vtp", 4 * h + grp + 1)
                    ins = None
                    if v:
                        for jj in range(4):
                            tt = grp * 4 + jj
                            ins = v.tensor_copy(Vaug[:, 129 * tt:129 * tt + 128],
                                                pb[2][:, 128 * jj:128 * (jj + 1)])
                            ins.annotate(f"vte h={h} grp={grp} jj={jj}")
                    p.inc("vte", ins)

                if h + 1 < H:
                    ternarize_head(h + 1)

                p.wait("tensor", "vte", p.cnt["vte"])
                head_base = ntile
                for gq in range(4):
                    nkt = 4 * gq + 4

                    def emit_qk(kt, tile):
                        diag = kt >= 4 * gq
                        qkbank = pb[3 + tile % 4]
                        if tile - head_base < 4:
                            p.wait("tensor", "nrme", 9 * h + 4 + tile % 4)
                        if tile >= 4:
                            p.wait("tensor", "exp", exp_after[tile - 4])
                        ins = t.matmul(qkbank[:, 0:512], Kh[:, 128 * kt:128 * (kt + 1)],
                                       Qh[:, 512 * gq:512 * (gq + 1)], start=True, stop=True) \
                            if t else None
                        p.inc("st_", ins)
                        p.wait("scalar", "st_", p.cnt["st_"])
                        p.wait("scalar", "negme", h + 1)
                        if tile >= 8:
                            p.wait("scalar", "av", tile - 7)
                        p.inc("exp", s.activation(ERING[tile % 8], pb[3 + tile % 4][:, 0:512],
                                                  AF.Exp,
                                                  bias=bc[:, BC_NEGM:BC_NEGM + 1],
                                                  scale=consts[:, CON_SCALE:CON_SCALE + 1])
                              if s else None)
                        exp_after[tile] = p.cnt["exp"]
                        if diag:
                            p.wait("vector", "exp", exp_after[tile])
                            p.inc("ste", v.tensor_mul(ERING[tile % 8], ERING[tile % 8],
                                                      MASKSB[:, 512 * (kt - 4 * gq):512 * (kt - 4 * gq + 1)])
                                  if v else None)
                            ste_after[tile] = p.cnt["ste"]

                    for w0 in range(min(3, nkt)):
                        emit_qk(w0, ntile + w0)
                    for kt in range(nkt):
                        tile = ntile
                        ntile += 1
                        if kt + 3 < nkt:
                            emit_qk(kt + 3, tile + 3)
                        p.wait("tensor", "exp", exp_after[tile])
                        if tile in ste_after:
                            p.wait("tensor", "ste", ste_after[tile])
                        ins = None
                        if t:
                            for sub in range(4):
                                if kt == 0 and ngq > 0:
                                    p.wait("tensor", "ave", 4 * ngq - 3 + sub)
                                obank = pb[[0, 1, 2, 7][sub]]
                                ins = t.matmul(obank[:, 0:129],
                                               ERING[tile % 8][:, 128 * sub:128 * (sub + 1)],
                                               Vaug[:, 129 * kt:129 * (kt + 1)],
                                               start=(kt == 0), stop=(kt == nkt - 1))
                        p.inc("av", ins)
                    ngq += 1
                    p.wait("vector", "av", p.cnt["av"])
                    for sub in range(4):
                        obank, ocol = pb[[0, 1, 2, 7][sub]], 0
                        ost_i = nost
                        nost += 1
                        if ost_i >= 4:
                            p.wait("vector", f"ostd{ost_i % 4}", 16 * (ost_i // 4))
                        ins = None
                        if v:
                            v.reciprocal(stats[:, S_RS + sub:S_RS + sub + 1],
                                         obank[:, ocol + 128:ocol + 129])
                            v.drain()
                            ins = v.tensor_scalar(OST[ost_i % 4], obank[:, ocol:ocol + 128],
                                                  stats[:, S_RS + sub:S_RS + sub + 1],
                                                  0.0, ALU.mult, ALU.add)
                        p.inc("ave", ins, 1)
                        p.wait("gpsimd", "ave", p.cnt["ave"])
                        q0 = 512 * gq + 128 * sub
                        p.inc(f"ostd{ost_i % 4}", g.dma_start(out=o_hbm[h, q0:q0 + 128, :],
                                                              in_=OST[ost_i % 4])
                              if g else None, 16)

            if "QK" in dbg:
                p.wait("gpsimd", "qkve")
                p.inc("dbgs", g.dma_start(out=dbg["QK"][:, 0:T], in_=Qh) if g else None, 16)
                p.inc("dbgs", g.dma_start(out=dbg["QK"][:, T:2 * T], in_=Kh) if g else None, 16)
            if "TERN" in dbg:
                p.wait("gpsimd", "terk")
                p.inc("dbgs", g.dma_start(out=dbg["TERN"][:], in_=TERN[:]) if g else None, 16)
            if "Vaug" in dbg:
                p.wait("gpsimd", "vte")
                p.inc("dbgs", g.dma_start(out=dbg["Vaug"][:], in_=Vaug) if g else None, 16)

        def p3(p: P):
            y = p.e if p.on("sync") else None
            g = p.e if p.on("gpsimd") else None
            t = p.e if p.on("tensor") else None
            v = p.e if p.on("vector") else None
            s = p.e if p.on("scalar") else None

            if s:
                s.preload_act_table(AF.Sqrt)

            for q in range(4):
                p.wait("gpsimd", f"ostd{q}")

            p.wait("vector", "pabs", 32)
            p.inc("vs3", v.reduce_sum(stats[:, S_MX:S_MX + 1], wabs2[:], axis=AX.X) if v else None)
            p.wait("tensor", "vs3")
            p.wait("tensor", "negme", H)
            p.inc("mmini", t.matmul(pb[7][0:1, 0:1], stats[:, S_MX:S_MX + 1], onesf[:, 0:1],
                                    start=True, stop=True) if t else None)
            p.wait("vector", "mmini")
            if v:
                v.tensor_scalar(sc1[0:1, 0:1], pb[7][0:1, 0:1], 1.0 / (C * C), 0.0, ALU.mult, ALU.add)
                v.drain()
            p.inc("vs3", v.tensor_scalar(sc1[0:1, 1:2], sc1[0:1, 0:1],
                                         1e-5, 0.0, ALU.max, ALU.add) if v else None)
            p.wait("tensor", "vs3")
            p.inc("mmini", t.matmul(pb[7][:, 0:1], ones1[:], sc1[0:1, 1:2],
                                    start=True, stop=True) if t else None)
            p.wait("vector", "mmini")
            ins = None
            if v:
                v.tensor_copy(bc[:, BC_MEANP:BC_MEANP + 1], pb[7][:, 0:1])
                v.drain()
                v.tensor_scalar(bc[:, BC_THP2:BC_THP2 + 1], bc[:, BC_MEANP:BC_MEANP + 1],
                                0.5, 0.0, ALU.mult, ALU.add)
                ins = v.tensor_scalar(bc[:, BC_THN2:BC_THN2 + 1], bc[:, BC_MEANP:BC_MEANP + 1],
                                      -0.5, 0.0, ALU.mult, ALU.add)
            p.inc("vs3", ins)

            for j in range(NCH):
                if y:
                    if j == 0:
                        y.wait_ge(sems["terk"], H)
                    if j >= 3:
                        y.wait_ge(sems["ternp"], 2 * (j - 2))
                p.inc(f"ldq{j % 3}", y.dma_start(out=WSLAB[:, 2048 * (j % 3):2048 * (j % 3 + 1)],
                                                 in_=wprojT_d[j, :, :]) if y else None, 16)
            for j in range(NCH):
                p.wait("vector", f"ldq{j % 3}", 16 * (j // 3 + 1))
                if j == 0:
                    p.wait("vector", "qkvg", 12 * H)
                    p.wait("vector", "nrme", 9 * H)
                    p.wait("vector", "vs3", p.cnt["vs3"])
                ins = None
                if v:
                    v.tensor_scalar(Q2B[:], WSLAB[:, 2048 * (j % 3):2048 * (j % 3 + 1)],
                                    bc[:, BC_THN2:BC_THN2 + 1], 0.0, ALU.is_lt, ALU.add)
                    v.drain()
                    ins = v.scalar_tensor_tensor(A1[:, T * j:T * j + 2048],
                                                 WSLAB[:, 2048 * (j % 3):2048 * (j % 3 + 1)],
                                                 bc[:, BC_THP2:BC_THP2 + 1], Q2B[:],
                                                 ALU.is_gt, ALU.subtract)
                    v.drain()
                p.inc("ternp", ins, 2)

            S3_M, S3_T2, S3_SR, S3_C, S3_R2M0, S3_R2M1, S3_NM, S3_TV, S3_M2, S3_VAR, S3_SD, S3_RN = \
                range(128, 140)
            AO3P = [f32v(A2, 0, C), f32v(A2, 4096, C)]
            X2S3 = f32v(A2, 8192, C)
            XRES = f32v(A2, 12288, C)
            OTR = [f32v(A2, 16384, C), f32v(A2, 20480, C), f32v(A2, 24576, C)]

            def p3_ln2x2(ic):
                p.wait("scalar", "proje", 4 * (ic + 1))
                p.wait("scalar", "x2st", ic)
                AO = AO3P[ic % 2]
                ins = None
                if s:
                    s.activation(GARB[:, 0:1024], AO[:, 0:1024], AF.Square,
                                 accum_out=stats[:, S3_M2:S3_M2 + 1])
                    s.drain()
                    ins = s.activation(GARB[:, 0:1024], AO[:, 1024:2048], AF.Square,
                                       accum_out=stats[:, S3_SD:S3_SD + 1])
                    s.drain()
                p.inc("sv3", ins)
                if v:
                    v.drain()
                    v.reduce_sum(stats[:, S3_NM:S3_NM + 1], AO[:], axis=AX.X)
                    v.drain()
                    v.tensor_scalar(stats[:, S3_NM:S3_NM + 1], stats[:, S3_NM:S3_NM + 1],
                                    -1.0 / C, 0.0, ALU.mult, ALU.add)
                    v.drain()
                p.wait("vector", "sv3", p.cnt["sv3"])
                if v:
                    v.tensor_add(stats[:, S3_TV:S3_TV + 1], stats[:, S3_M2:S3_M2 + 1],
                                 stats[:, S3_SD:S3_SD + 1])
                    v.drain()
                    v.tensor_scalar(stats[:, S3_TV:S3_TV + 1], stats[:, S3_TV:S3_TV + 1],
                                    1.0 / C, 1e-5, ALU.mult, ALU.add)
                    v.drain()
                    v.tensor_mul(stats[:, S3_M2:S3_M2 + 1], stats[:, S3_NM:S3_NM + 1],
                                 stats[:, S3_NM:S3_NM + 1])
                    v.drain()
                p.inc("vs3", v.tensor_sub(stats[:, S3_VAR:S3_VAR + 1],
                                          stats[:, S3_TV:S3_TV + 1],
                                          stats[:, S3_M2:S3_M2 + 1]) if v else None)
                p.wait("scalar", "vs3", p.cnt["vs3"])
                p.inc("sv3", s.activation(stats[:, S3_SD:S3_SD + 1],
                                          stats[:, S3_VAR:S3_VAR + 1], AF.Sqrt) if s else None)
                if y and ic >= 1:
                    y.wait_ge(sems["x2st"], ic)
                p.inc(f"ldx{ic % 2}", y.dma_start(out=XRES, in_=x_d[128 * ic:128 * (ic + 1), :])
                      if y else None, 16)
                ldx_tgt = p.cnt[f"ldx{ic % 2}"]
                p.wait("vector", "sv3", p.cnt["sv3"])
                if v:
                    v.reciprocal(stats[:, S3_RN:S3_RN + 1], stats[:, S3_SD:S3_SD + 1])
                    v.drain()
                    v.tensor_scalar(TMP[:], AO[:], stats[:, S3_NM:S3_NM + 1],
                                    stats[:, S3_RN:S3_RN + 1], ALU.add, ALU.mult)
                    v.drain()
                if ic >= 1:
                    p.wait("vector", f"ostd{(ic - 1) % 2}", p.cnt[f"ostd{(ic - 1) % 2}"])
                p.wait("vector", f"ldx{ic % 2}", ldx_tgt)
                p.inc("x2st", v.tensor_add(X2S3, TMP[:], XRES) if v else None)
                p.wait("gpsimd", "x2st", p.cnt["x2st"])
                p.inc(f"ostd{ic % 2}", g.dma_start(out=x2_hbm[128 * ic:128 * (ic + 1), :],
                                                   in_=X2S3) if g else None, 16)

            def p3_gather(i):
                if g and i >= 3:
                    g.wait_ge(sems["oq"], i - 2)
                for hh2 in range(2):
                    ins = None
                    if g:
                        srcg = bass.AP(o_hbm, (8 * hh2) * T * D + 128 * i * D,
                                       [[D, 128], [T * D, 8], [1, D]])
                        dstg = OTR[i % 3][:, 1024 * hh2:1024 * (hh2 + 1)].rearrange(
                            "p (h d) -> p h d", d=D)
                        ins = g.dma_start(out=dstg, in_=srcg)
                    p.inc(f"ldo{i % 3}", ins, 16)

            def p3_quant(i):
                p.wait("vector", f"ldo{i % 3}", 32 * (i // 3 + 1))
                if i == 0:
                    p.wait("vector", "ternp", 2 * NCH)
                r2m = S3_R2M0 if i % 2 == 0 else S3_R2M1
                OT = OTR[i % 3]
                if v:
                    v.reduce_max(stats[:, S3_M:S3_M + 1], OT, axis=AX.X,
                                 apply_absolute_value=True)
                    v.drain()
                    v.tensor_scalar(stats[:, S3_T2:S3_T2 + 1], stats[:, S3_M:S3_M + 1],
                                    1e-5, 0.0, ALU.max, ALU.add)
                    v.drain()
                    v.reciprocal(stats[:, S3_SR:S3_SR + 1], stats[:, S3_T2:S3_T2 + 1])
                    v.drain()
                    v.tensor_scalar(stats[:, S3_C:S3_C + 1], stats[:, S3_SR:S3_SR + 1],
                                    127.0, 0.0, ALU.mult, ALU.add)
                    v.tensor_scalar(stats[:, r2m:r2m + 1], stats[:, S3_T2:S3_T2 + 1],
                                    bc[:, BC_MEANP:BC_MEANP + 1], 1.0 / 127.0, ALU.mult, ALU.mult)
                    v.drain()
                    v.tensor_scalar(TMP[:], OT, stats[:, S3_C:S3_C + 1],
                                    127.49999237060547, ALU.mult, ALU.min)
                    v.drain()
                    v.tensor_scalar(QF16[:], TMP[:], -128.5, 1536.0, ALU.max, ALU.add)
                    v.drain()
                p.inc("oq", v.tensor_scalar(XQBF[:], QF16[:], -1536.0, 0.0, ALU.add, ALU.add)
                      if v else None)

            p3_gather(0)
            p3_gather(1)
            p3_quant(0)
            for i in range(NTOK):
                if i + 2 < NTOK:
                    p3_gather(i + 2)

                for grp in range(4):
                    gidx = 4 * i + grp
                    p.wait("tensor", "oq", i + 1)
                    if gidx < 2:
                        p.wait("tensor", "ave", 16 * H)
                    else:
                        p.wait("tensor", "oqTe", gidx - 1)
                    ins = None
                    if t:
                        for jj in range(4):
                            j = grp * 4 + jj
                            ins = t.transpose(pb[gidx % 2][:, 64 * jj:64 * (jj + 1)].bitcast(BF16),
                                              XQBF[:, 128 * j:128 * (j + 1)], identb[:])
                    p.inc("oqT", ins)
                    p.wait("vector", "oqT", p.cnt["oqT"])
                    ins = None
                    if v:
                        for jj in range(4):
                            j = grp * 4 + jj
                            ins = v.tensor_copy(Q2B[:, 128 * j:128 * (j + 1)],
                                                pb[gidx % 2][:, 64 * jj:64 * (jj + 1)].bitcast(BF16))
                    p.inc("oqTe", ins)

                if i + 1 < NTOK:
                    p3_quant(i + 1)

                for fg in range(4):
                    if i == 0:
                        p.wait("tensor", "ternp", 2 * NCH)
                    if i >= 1:
                        p.wait("tensor", "proje", 4 * (i - 1) + fg + 1)
                    p.wait("tensor", "oqTe", 4 * (i + 1))
                    ins = None
                    if t:
                        for j in range(NCH):
                            ins = t.matmul(pb[2 + fg][:, 0:512], Q2B[:, 128 * j:128 * (j + 1)],
                                           A1[:, T * j + 512 * fg:T * j + 512 * (fg + 1)],
                                           start=(j == 0), stop=(j == NCH - 1))
                    p.inc("projg", ins)
                    p.wait("vector", "projg", p.cnt["projg"])
                    r2mi = S3_R2M0 if i % 2 == 0 else S3_R2M1
                    p.inc("proje", v.tensor_scalar(AO3P[i % 2][:, 512 * fg:512 * (fg + 1)],
                                                   pb[2 + fg][:, 0:512],
                                                   stats[:, r2mi:r2mi + 1], 0.0,
                                                   ALU.mult, ALU.add) if v else None)

                if i >= 1:
                    p3_ln2x2(i - 1)
            p3_ln2x2(NTOK - 1)

        def p4(p: P):
            y = p.e if p.on("sync") else None
            g = p.e if p.on("gpsimd") else None
            t = p.e if p.on("tensor") else None
            v = p.e if p.on("vector") else None
            s = p.e if p.on("scalar") else None

            NS = 16
            FS = 4
            LHI = A1
            LLO = A2
            LO3 = QF16[:].bitcast(BF16)
            GHI = WSLAB[:, 0:4096].bitcast(BF16)
            GLOa = WSLAB[:, 4096:6144].bitcast(BF16)
            GLOb = TMP[:].bitcast(BF16)
            WGHI = XT[0][:, 0:1024].bitcast(BF16)
            WGLO = XT[0][:, 1024:2048].bitcast(BF16)
            WUHI = XT[1][:, 0:1024].bitcast(BF16)
            WULO = XT[1][:, 1024:2048].bitcast(BF16)
            WDBUF = [XQBF[:, 0:1024].bitcast(F32), XQBF[:, 1024:2048].bitcast(F32)]
            WDHP = [MASKSB, WGHI]
            WDLP = [LO3, WGLO]
            GTMP = GARB[:, 0:512]
            STG = [GARB[:, 0:1024], SCR[:, 0:1024]]

            def glo(jc):
                return GLOa[:, 2048 * jc:2048 * (jc + 1)] if jc < 2 else \
                    GLOb[:, 2048 * (jc - 2):2048 * (jc - 1)]

            if s:
                s.preload_act_table(AF.Sqrt)
            p.wait("sync", "ostd0")
            p.wait("sync", "ostd1")

            S4 = dict(NM=128, TV=129, M2=130, VAR=131, SD=132, RN=133)
            for i in range(NTOK):
                if y and i >= 2:
                    y.wait_ge(sems["l3"], 2 * (i - 1))
                p.inc(f"ldx{i % 2}", y.dma_start(out=XT[i % 2][:],
                                                 in_=x2_hbm[128 * i:128 * (i + 1), :])
                      if y else None, 16)
                ldt = p.cnt[f"ldx{i % 2}"]
                p.wait("scalar", f"ldx{i % 2}", ldt)
                p.wait("scalar", "l3", 2 * i)
                ins = None
                if s:
                    s.activation(GARB[:], XT[i % 2][:, 0:1024], AF.Square,
                                 accum_out=stats[:, S4["M2"]:S4["M2"] + 1])
                    s.drain()
                    ins = s.activation(GARB[:], XT[i % 2][:, 1024:2048], AF.Square,
                                       accum_out=stats[:, S4["SD"]:S4["SD"] + 1])
                    s.drain()
                p.inc("sv4", ins)
                p.wait("vector", f"ldx{i % 2}", ldt)
                if v:
                    v.reduce_sum(stats[:, S4["NM"]:S4["NM"] + 1], XT[i % 2][:], axis=AX.X)
                    v.drain()
                    v.tensor_scalar(stats[:, S4["NM"]:S4["NM"] + 1],
                                    stats[:, S4["NM"]:S4["NM"] + 1], -1.0 / C, 0.0,
                                    ALU.mult, ALU.add)
                    v.drain()
                p.wait("vector", "sv4", p.cnt["sv4"])
                if v:
                    v.tensor_add(stats[:, S4["TV"]:S4["TV"] + 1],
                                 stats[:, S4["M2"]:S4["M2"] + 1],
                                 stats[:, S4["SD"]:S4["SD"] + 1])
                    v.drain()
                    v.tensor_scalar(stats[:, S4["TV"]:S4["TV"] + 1],
                                    stats[:, S4["TV"]:S4["TV"] + 1], 1.0 / C, 1e-5,
                                    ALU.mult, ALU.add)
                    v.drain()
                    v.tensor_mul(stats[:, S4["M2"]:S4["M2"] + 1],
                                 stats[:, S4["NM"]:S4["NM"] + 1],
                                 stats[:, S4["NM"]:S4["NM"] + 1])
                    v.drain()
                p.inc("vs4", v.tensor_sub(stats[:, S4["VAR"]:S4["VAR"] + 1],
                                          stats[:, S4["TV"]:S4["TV"] + 1],
                                          stats[:, S4["M2"]:S4["M2"] + 1]) if v else None)
                p.wait("scalar", "vs4", p.cnt["vs4"])
                p.inc("sv4", s.activation(stats[:, S4["SD"]:S4["SD"] + 1],
                                          stats[:, S4["VAR"]:S4["VAR"] + 1], AF.Sqrt)
                      if s else None)
                p.wait("vector", "sv4", p.cnt["sv4"])
                if v:
                    v.reciprocal(stats[:, S4["RN"]:S4["RN"] + 1], stats[:, S4["SD"]:S4["SD"] + 1])
                    v.drain()
                    v.tensor_scalar(TMP[:], XT[i % 2][:], stats[:, S4["NM"]:S4["NM"] + 1],
                                    stats[:, S4["RN"]:S4["RN"] + 1], ALU.add, ALU.mult)
                    v.drain()
                    v.tensor_copy(XQBF[:], TMP[:])
                    v.drain()
                p.inc("l3", v.tensor_tensor(LO3, TMP[:], XQBF[:], ALU.subtract)
                      if v else None, 2)
                for half in range(2):
                    for grp in range(4):
                        gidx = 8 * i + 4 * half + grp
                        p.wait("tensor", "l3", 2 * (i + 1))
                        if gidx >= 2:
                            p.wait("tensor", "l3te", gidx - 1)
                        elif "p3" in phases:
                            p.wait("tensor", "proje", 64)
                        ins = None
                        if t:
                            for jj in range(4):
                                j = grp * 4 + jj
                                sap = XQBF[:, 128 * j:128 * (j + 1)] if half == 0 else \
                                    LO3[:, 128 * j:128 * (j + 1)]
                                ins = t.transpose(pb[gidx % 2][:, 64 * jj:64 * (jj + 1)].bitcast(BF16),
                                                  sap, identb[:])
                        p.inc("l3tp", ins)
                        p.wait("vector", "l3tp", p.cnt["l3tp"])
                        dslab = LHI if half == 0 else LLO
                        ins = None
                        if v:
                            for jj in range(4):
                                j = grp * 4 + jj
                                ins = v.tensor_copy(dslab[:, T * j + 128 * i:T * j + 128 * (i + 1)],
                                                    pb[gidx % 2][:, 64 * jj:64 * (jj + 1)].bitcast(BF16))
                        p.inc("l3te", ins)

            p.wait("scalar", "sv4")
            p.inc("sv4", s.preload_act_table(AF.Silu) if s else None)

            ldcnt = {}

            def emit_wload(ft, which):
                wsrc = wgT_d if which == 0 else wuT_d
                sem_l = "ldwg" if which == 0 else "ldwu"
                if y and 2 * ft + which >= 1:
                    y.wait_ge(sems["wsp"], 4 * ft + 2 * which)
                for qq in range(4):
                    src = bass.AP(wsrc, ft * C * 128 + 128 * 128 * 4 * qq,
                                  [[128, 128], [128 * 128, 4], [1, 128]])
                    dst = bass.AP(R1REP, 128 * 4 * qq, [[2048, 128], [128, 4], [1, 128]])
                    p.inc(sem_l, y.dma_start(out=dst, in_=src) if y else None, 16)
                ldcnt[(ft, which)] = p.cnt[sem_l]

            def emit_wcopy(ft, which):
                sem_l = "ldwg" if which == 0 else "ldwu"
                p.wait("vector", sem_l, ldcnt[(ft, which)])
                if ft > 0:
                    p.wait("vector", "gu", 8 * ft - (1 if which == 0 else 0))
                if which == 0 and ft % FS == 0 and ft > 0:
                    p.wait("vector", "dn", 64 * (ft // FS))
                hi = WGHI if which == 0 else WUHI
                lo = WGLO if which == 0 else WULO
                if v:
                    v.drain()
                p.inc("wsp", v.tensor_copy(hi, R1REP[:]) if v else None)
                if v:
                    v.drain()
                p.inc("wsp", v.tensor_tensor(lo, R1REP[:], hi, ALU.subtract)
                      if v else None)
                if v:
                    v.drain()

            def emit_wpair(ft, which):
                emit_wcopy(ft, which)
                nxt = 2 * ft + which + 1
                if nxt < 2 * NS * FS:
                    emit_wload(nxt // 2, nxt % 2)

            ngu = 0
            ndn = 0
            nwd = 0
            for sp in range(NS):
                for fl in range(FS):
                    ft = FS * sp + fl
                    if ft == 0:
                        emit_wload(0, 0)
                        emit_wpair(0, 0)
                        emit_wpair(0, 1)
                    elif fl == 0:
                        emit_wpair(ft, 0)
                        emit_wpair(ft, 1)
                    for tokg in range(4):
                        for which in range(2):
                            gidx = ngu
                            ngu += 1
                            bank = pb[(2 if which == 0 else 3) + 2 * (tokg % 2)]
                            p.wait("tensor", "wsp", 4 * ft + 2 * which + 1)
                            if fl == 0 and tokg <= 1:
                                p.wait("tensor", "dne", 64 * sp)
                            if gidx >= 4:
                                p.wait("tensor", "gsp", (gidx - 4) // 2 + 1)
                            elif "p3" in phases:
                                p.wait("tensor", "proje", 64)
                            hi = WGHI if which == 0 else WUHI
                            lo = WGLO if which == 0 else WULO
                            ins = None
                            if t:
                                for j in range(NCH):
                                    ins = t.matmul(bank[:, 0:512],
                                                   hi[:, 128 * j:128 * (j + 1)],
                                                   LHI[:, T * j + 512 * tokg:T * j + 512 * (tokg + 1)],
                                                   start=(j == 0), stop=False)
                                for j in range(NCH):
                                    ins = t.matmul(bank[:, 0:512],
                                                   hi[:, 128 * j:128 * (j + 1)],
                                                   LLO[:, T * j + 512 * tokg:T * j + 512 * (tokg + 1)],
                                                   start=False, stop=False)
                            p.wait("tensor", "wsp", 4 * ft + 2 * which + 2)
                            ins = None
                            if t:
                                for j in range(NCH):
                                    ins = t.matmul(bank[:, 0:512],
                                                   lo[:, 128 * j:128 * (j + 1)],
                                                   LHI[:, T * j + 512 * tokg:T * j + 512 * (tokg + 1)],
                                                   start=False, stop=(j == NCH - 1))
                            p.inc("gu", ins)
                        gbank = pb[2 + 2 * (tokg % 2)]
                        ubank = pb[3 + 2 * (tokg % 2)]
                        p.wait("scalar", "gu", p.cnt["gu"])
                        p.wait("scalar", "sv4", p.cnt["sv4"])
                        if fl == 0 and tokg == 0:
                            for _sl in range(2):
                                p.wait("scalar", f"accw{_sl}", p.cnt[f"accw{_sl}"])
                        if ngu // 2 - 1 > 0:
                            p.wait("scalar", "gsp", ngu // 2 - 1)
                        p.inc("sil", s.activation(GTMP, gbank[:, 0:512], AF.Silu)
                              if s else None)
                        p.wait("vector", "sil", p.cnt["sil"])
                        if sp > 0 and fl == 0 and tokg == 0:
                            p.wait("vector", "dn", 64 * sp)
                        ins = None
                        if v:
                            v.tensor_tensor(GTMP, GTMP, ubank[:, 0:512], ALU.mult)
                            v.drain()
                            v.tensor_copy(GHI[:, 2048 * fl + 512 * tokg:2048 * fl + 512 * (tokg + 1)],
                                          GTMP)
                            v.drain()
                            ins = v.tensor_tensor(glo(fl)[:, 512 * tokg:512 * (tokg + 1)],
                                                  GTMP,
                                                  GHI[:, 2048 * fl + 512 * tokg:2048 * fl + 512 * (tokg + 1)],
                                                  ALU.subtract)
                            v.drain()
                        p.inc("gsp", ins)
                        if tokg == 2 and fl < FS - 1:
                            emit_wpair(ft + 1, 0)
                    if fl < FS - 1:
                        emit_wpair(ft + 1, 1)

                for cg in range(4):
                    cg_glob = 4 * sp + cg
                    WDHI = WDHP[cg % 2]
                    WDLO = WDLP[cg % 2]
                    for pair in range(2):
                        for jj in range(2):
                            jc = 2 * pair + jj
                            k = nwd
                            nwd += 1
                            bi = k % 2
                            if y and k >= 2:
                                _j2 = (k - 2) % 4
                                y.wait_ge(sems["wdsp"],
                                          8 * ((k - 2) // 4) + [3, 4, 7, 8][_j2])
                            src = wdT_d[512 * sp + 128 * jc:512 * sp + 128 * (jc + 1),
                                        512 * cg:512 * (cg + 1)]
                            p.inc(f"ldwd{bi}", y.dma_start(out=WDBUF[bi], in_=src)
                                  if y else None, 16)
                            p.wait("vector", f"ldwd{bi}", p.cnt[f"ldwd{bi}"])
                            if jc == 0:
                                p.wait("vector", "dn", max(0, p.cnt["dn"] - 16))
                            if v:
                                v.drain()
                            p.inc("wdsp", v.tensor_copy(WDHI[:, 512 * jc:512 * (jc + 1)],
                                                        WDBUF[bi]) if v else None)
                        for jj in range(2):
                            jc = 2 * pair + jj
                            if v:
                                v.drain()
                            p.inc("wdsp", v.tensor_tensor(WDLO[:, 512 * jc:512 * (jc + 1)],
                                                          WDBUF[jc % 2],
                                                          WDHI[:, 512 * jc:512 * (jc + 1)],
                                                          ALU.subtract) if v else None)
                            if v:
                                v.drain()
                    for tt in range(NTOK):
                        gidx = ndn
                        ndn += 1
                        bank = pb[2 + gidx % 6]
                        base = 8 * cg_glob
                        p.wait("tensor", "gsp", p.cnt["gsp"])
                        if gidx >= 6:
                            p.wait("tensor", "dne", gidx - 5)
                        ins = None
                        WDHIb = WDHP[cg % 2]
                        WDLOb = WDLP[cg % 2]
                        for jc in range(4):
                            if tt == 0:
                                p.wait("tensor", "wdsp",
                                       base + (jc + 1 if jc < 2 else jc + 3))
                            ins = t.matmul(bank[:, 0:512],
                                           GHI[:, 2048 * jc + 128 * tt:2048 * jc + 128 * (tt + 1)],
                                           WDHIb[:, 512 * jc:512 * (jc + 1)],
                                           start=(jc == 0), stop=False) if t else None
                        for jc in range(4):
                            ins = t.matmul(bank[:, 0:512],
                                           glo(jc)[:, 128 * tt:128 * (tt + 1)],
                                           WDHIb[:, 512 * jc:512 * (jc + 1)],
                                           start=False, stop=False) if t else None
                        for jc in range(4):
                            if tt == 0:
                                p.wait("tensor", "wdsp",
                                       base + (jc + 3 if jc < 2 else jc + 5))
                            ins = t.matmul(bank[:, 0:512],
                                           GHI[:, 2048 * jc + 128 * tt:2048 * jc + 128 * (tt + 1)],
                                           WDLOb[:, 512 * jc:512 * (jc + 1)],
                                           start=False, stop=(jc == 3)) if t else None
                        p.inc("dn", ins)
                        pair = (gidx // 2) % 2
                        half = gidx % 2
                        p.wait("scalar", "dn", p.cnt["dn"])
                        if half == 0 and gidx >= 4:
                            p.wait("scalar", f"accw{pair}", p.cnt[f"accw{pair}"])
                        p.inc("dne", s.copy(STG[pair][:, 512 * half:512 * (half + 1)],
                                            bank[:, 0:512]) if s else None)
                        if s:
                            s.drain()
                        if half == 1:
                            p.wait("gpsimd", "dne", p.cnt["dne"])
                            dst = bass.AP(acc_hbm, 128 * (tt - 1) * C + 512 * cg,
                                          [[C, 128], [128 * C, 2], [1, 512]])
                            ins = None
                            if g:
                                if sp == 0:
                                    ins = g.dma_start(out=dst, in_=STG[pair])
                                else:
                                    ins = g.dma_start(out=dst, in_=STG[pair], accum_op=ALU.add)
                            p.inc(f"accw{pair}", ins, 16)

            p.wait("scalar", "sv4")
            p.inc("sv4", s.preload_act_table(AF.Sqrt) if s else None)
            p.wait("sync", "accw0")
            p.wait("sync", "accw1")
            OF = [WSLAB[:, 2048:4096], WSLAB[:, 4096:6144]]
            LN4T = WSLAB[:, 0:2048]
            for i in range(NTOK):
                if y:
                    if i >= 2:
                        y.wait_ge(sems["x2st"], NTOK + i - 1)
                p.inc(f"ldac{i % 2}", y.dma_start(out=XT[i % 2][:],
                                                  in_=acc_hbm[128 * i:128 * (i + 1), :])
                      if y else None, 16)
                if y and i >= 1:
                    y.wait_ge(sems["x2st"], NTOK + i)
                p.inc(f"ldac{i % 2}", y.dma_start(out=TMP[:],
                                                  in_=x2_hbm[128 * i:128 * (i + 1), :])
                      if y else None, 16)
                ldt = p.cnt[f"ldac{i % 2}"]
                p.wait("scalar", f"ldac{i % 2}", ldt)
                p.wait("scalar", "x2st", NTOK + i)
                ins = None
                if s:
                    s.activation(GARB[:], XT[i % 2][:, 0:1024], AF.Square,
                                 accum_out=stats[:, S4["M2"]:S4["M2"] + 1])
                    s.drain()
                    ins = s.activation(GARB[:], XT[i % 2][:, 1024:2048], AF.Square,
                                       accum_out=stats[:, S4["SD"]:S4["SD"] + 1])
                    s.drain()
                p.inc("sv4", ins)
                p.wait("vector", f"ldac{i % 2}", ldt)
                if v:
                    v.reduce_sum(stats[:, S4["NM"]:S4["NM"] + 1], XT[i % 2][:], axis=AX.X)
                    v.drain()
                    v.tensor_scalar(stats[:, S4["NM"]:S4["NM"] + 1],
                                    stats[:, S4["NM"]:S4["NM"] + 1], -1.0 / C, 0.0,
                                    ALU.mult, ALU.add)
                    v.drain()
                p.wait("vector", "sv4", p.cnt["sv4"])
                if v:
                    v.tensor_add(stats[:, S4["TV"]:S4["TV"] + 1],
                                 stats[:, S4["M2"]:S4["M2"] + 1],
                                 stats[:, S4["SD"]:S4["SD"] + 1])
                    v.drain()
                    v.tensor_scalar(stats[:, S4["TV"]:S4["TV"] + 1],
                                    stats[:, S4["TV"]:S4["TV"] + 1], 1.0 / C, 1e-5,
                                    ALU.mult, ALU.add)
                    v.drain()
                    v.tensor_mul(stats[:, S4["M2"]:S4["M2"] + 1],
                                 stats[:, S4["NM"]:S4["NM"] + 1],
                                 stats[:, S4["NM"]:S4["NM"] + 1])
                    v.drain()
                p.inc("vs4", v.tensor_sub(stats[:, S4["VAR"]:S4["VAR"] + 1],
                                          stats[:, S4["TV"]:S4["TV"] + 1],
                                          stats[:, S4["M2"]:S4["M2"] + 1]) if v else None)
                p.wait("scalar", "vs4", p.cnt["vs4"])
                p.inc("sv4", s.activation(stats[:, S4["SD"]:S4["SD"] + 1],
                                          stats[:, S4["VAR"]:S4["VAR"] + 1], AF.Sqrt)
                      if s else None)
                p.wait("vector", "sv4", p.cnt["sv4"])
                if v:
                    v.reciprocal(stats[:, S4["RN"]:S4["RN"] + 1], stats[:, S4["SD"]:S4["SD"] + 1])
                    v.drain()
                    v.tensor_scalar(LN4T, XT[i % 2][:], stats[:, S4["NM"]:S4["NM"] + 1],
                                    stats[:, S4["RN"]:S4["RN"] + 1], ALU.add, ALU.mult)
                    v.drain()
                if i >= 2:
                    p.wait("vector", f"outw{i % 2}", 16 * (i // 2))
                p.inc("x2st", v.tensor_add(OF[i % 2], LN4T, TMP[:]) if v else None)
                p.wait("gpsimd", "x2st", p.cnt["x2st"])
                p.inc(f"outw{i % 2}", g.dma_start(out=out_d[128 * i:128 * (i + 1), :],
                                                  in_=OF[i % 2]) if g else None, 16)
            p.wait("gpsimd", "outw0")
            p.wait("gpsimd", "outw1")

        def emit_all(p: P):
            if "p0" in phases:
                p0(p)
            if "p12" in phases:
                p12(p)
            if "p3" in phases:
                p3(p)
            if "p4" in phases:
                p4(p)
            g = p.e if p.on("gpsimd") else None
            if dbg or True:
                p.wait("gpsimd", "evac")
                p.wait("gpsimd", "xqbf")
                if "xqT" in dbg:
                    p.inc("dbgs", g.dma_start(out=dbg["xqT"][:], in_=A1[:]) if g else None, 16)
                if "r1" in dbg:
                    p.inc("dbgs", g.dma_start(out=dbg["r1"][:], in_=r1c[:]) if g else None, 16)
                if "wmean" in dbg:
                    p.inc("dbgs", g.dma_start(out=dbg["wmean"][:], in_=bc[:, 0:2]) if g else None, 16)
                if "R1rep" in dbg:
                    p.inc("dbgs", g.dma_start(out=dbg["R1rep"][:], in_=R1REP[:]) if g else None, 16)
                p.wait("gpsimd", "dbgs")
                for q in range(4):
                    p.wait("gpsimd", f"ostd{q}")

        for engname in ENGS:
            def body(eng, engname=engname):
                emit_all(P(engname, eng, sems))
            getattr(block, engname)(body)

    return nc, dbg

N_CORES = 8


import ml_dtypes
from concourse.bass_utils import run_bass_kernel_spmd


def _layout_common(inp):
    wq = np.ascontiguousarray(inp["L0_attn_w_qkv"], dtype=np.float32)
    wp = np.ascontiguousarray(inp["L0_attn_w_proj"], dtype=np.float32)
    wg = np.ascontiguousarray(inp["L0_ffn_w_gate"], dtype=np.float32)
    wu = np.ascontiguousarray(inp["L0_ffn_w_up"], dtype=np.float32)
    wd = np.ascontiguousarray(inp["L0_ffn_w_down"], dtype=np.float32)
    common = {
        "wqkvT": np.ascontiguousarray(wq.T).reshape(NCH, 128, 3 * C),
        "wprojT": np.ascontiguousarray(wp.T).reshape(NCH, 128, C),
        "wgT": np.ascontiguousarray(wg.reshape(HID // 128, 128, C).transpose(0, 2, 1)),
        "wuT": np.ascontiguousarray(wu.reshape(HID // 128, 128, C).transpose(0, 2, 1)),
        "wdT": np.ascontiguousarray(wd.T),
        "identf": np.eye(128, dtype=np.float32),
        "identb": np.eye(128, dtype=ml_dtypes.bfloat16),
        "ones": np.ones((128, 2), np.float32),
    }
    masks = np.empty((128, 4 * 512), ml_dtypes.bfloat16)
    for i in range(4):
        pp = np.arange(128)[:, None] + 128 * i
        ql = np.arange(512)[None, :]
        masks[:, 512 * i:512 * (i + 1)] = np.where(pp <= ql, 1.0, 0.0).astype(ml_dtypes.bfloat16)
    common["masks"] = masks
    consts = np.zeros((128, NCONST), np.float32)
    consts[:, CON_EPS] = 1e-5
    consts[:, CON_SCALE] = SCALE
    common["consts"] = consts
    return common


_NC_CACHE = {}


def kernel(L0_attn_x, L0_attn_w_qkv, L0_attn_w_proj, L0_ffn_w_gate, L0_ffn_w_up,
           L0_ffn_w_down):
    x = np.ascontiguousarray(np.asarray(L0_attn_x), dtype=np.float32)
    common = _layout_common(dict(
        L0_attn_w_qkv=np.asarray(L0_attn_w_qkv), L0_attn_w_proj=np.asarray(L0_attn_w_proj),
        L0_ffn_w_gate=np.asarray(L0_ffn_w_gate), L0_ffn_w_up=np.asarray(L0_ffn_w_up),
        L0_ffn_w_down=np.asarray(L0_ffn_w_down)))
    if "nc" not in _NC_CACHE:
        _NC_CACHE["nc"] = build()[0]
    nc = _NC_CACHE["nc"]
    in_maps = []
    for c in range(N_CORES):
        m = dict(common)
        m["x"] = np.ascontiguousarray(x[c])
        in_maps.append(m)
    res = run_bass_kernel_spmd(nc, in_maps, core_ids=list(range(N_CORES)))
    return np.stack([np.asarray(res.results[c]["out"]) for c in range(N_CORES)], axis=0)


# version 12
# speedup vs baseline: 1.1032x; 1.1032x over previous
import contextlib
import math
import numpy as np
import concourse.bass as bass
import concourse.mybir as mybir

F32 = mybir.dt.float32
BF16 = mybir.dt.bfloat16
I8 = mybir.dt.int8
AF = mybir.ActivationFunctionType
ALU = mybir.AluOpType
AX = mybir.AxisListType

C = 2048
T = 2048
H = 16
D = 128
HID = 8192
NCH = C // 128
NTOK = T // 128
SCALE = D ** -0.5
NEG = -1.0e9

CON_EPS = 0
CON_SCALE = 1
NCONST = 2

ENGS = ["sync", "tensor", "vector", "scalar", "gpsimd"]


class P:
    def __init__(self, active, eng, sems):
        self.a = active
        self.e = eng
        self.sems = sems
        self.cnt = {k: 0 for k in sems}

    def on(self, name):
        return self.a == name

    def inc(self, sem, instr, n=1):
        self.cnt[sem] += n
        if instr is not None:
            instr.then_inc(self.sems[sem], n)
        return instr

    def wait(self, eng, sem, val=None):
        v = self.cnt[sem] if val is None else val
        if self.a == eng and v > 0:
            self.e.wait_ge(self.sems[sem], v)


SEMS = [
    "pre", "ldx0", "ldx1", "ldw0", "ldw1", "xsq", "rsq", "var", "chunk", "xqbf",
    "tpose", "evac", "wabs", "wsum", "mmini", "r1T", "gb",
    "ldh", "terk", "qkvg", "qkve", "vtp", "vte", "nrm2", "nrme", "negme",
    "st_", "ste", "exp", "av", "ave", "ost",
    "ostd0", "ostd1", "ostd2", "ostd3", "ldp0", "ldp1", "pabs",
    "ldo0", "ldo1", "ldo2", "ldq0", "ldq1", "ldq2", "ternp", "oq", "oqT", "oqTe", "projg", "proje",
    "ldx3", "x2st", "vs3", "sv3",
    "l3", "l3tp", "l3te", "ldwg", "ldwu", "wsp", "gu", "sil", "gsp",
    "ldwd0", "ldwd1", "wdsp", "dn", "dne", "accw0", "accw1", "accw2",
    "ldac0", "ldac1",
    "outw0", "outw1", "vs4", "sv4",
    "dbgs",
]


def build(debug=(), phases=("p0", "p12", "p3", "p4")):
    nc = bass.Bass("TRN2", target_bir_lowering=False)

    x_d = nc.dram_tensor("x", [T, C], F32, kind="ExternalInput")
    wqkvT_d = nc.dram_tensor("wqkvT", [NCH, 128, 3 * C], F32, kind="ExternalInput")
    wprojT_d = nc.dram_tensor("wprojT", [NCH, 128, C], F32, kind="ExternalInput")
    wgT_d = nc.dram_tensor("wgT", [HID // 128, C, 128], F32, kind="ExternalInput")
    wuT_d = nc.dram_tensor("wuT", [HID // 128, C, 128], F32, kind="ExternalInput")
    wdT_d = nc.dram_tensor("wdT", [HID, C], F32, kind="ExternalInput")
    identf_d = nc.dram_tensor("identf", [128, 128], F32, kind="ExternalInput")
    identb_d = nc.dram_tensor("identb", [128, 128], BF16, kind="ExternalInput")
    masks_d = nc.dram_tensor("masks", [128, 4 * 512], BF16, kind="ExternalInput")
    consts_d = nc.dram_tensor("consts", [128, NCONST], F32, kind="ExternalInput")
    ones_d = nc.dram_tensor("ones", [128, 2], F32, kind="ExternalInput")

    out_d = nc.dram_tensor("out", [T, C], F32, kind="ExternalOutput")
    o_hbm = nc.dram_tensor("o_scr", [H, T, D], F32,
                           kind="ExternalOutput" if "O" in debug else "Internal")
    x2_hbm = nc.dram_tensor("x2_scr", [T, C], F32,
                            kind="ExternalOutput" if "x2" in debug else "Internal")
    acc_hbm = nc.dram_tensor("acc_scr", [T, C], F32, kind="Internal")
    r1_hbm = nc.dram_tensor("r1_scr", [NTOK, 128], F32, kind="Internal")

    dbg = {}

    def dbg_out(name, shape, dtype=F32):
        if name in debug:
            dbg[name] = nc.dram_tensor("dbg_" + name, shape, dtype, kind="ExternalOutput")
        return dbg.get(name)

    dbg_out("xqT", [128, NCH * T], BF16)
    dbg_out("r1", [128, NTOK])
    dbg_out("wmean", [128, 2])
    dbg_out("R1rep", [128, T])
    dbg_out("QK", [128, 2 * T])
    dbg_out("Vaug", [128, NTOK * 129])
    dbg_out("TERN", [128, 3 * C], BF16)

    with contextlib.ExitStack() as ctx:
        E = ctx.enter_context
        sems = {n: E(nc.semaphore(n)) for n in SEMS}

        A1 = E(nc.sbuf_tensor("sb_A1", [128, 32768], BF16))
        A2 = E(nc.sbuf_tensor("sb_A2", [128, 32768], BF16))
        WSLAB = E(nc.sbuf_tensor("sb_WSLAB", [128, 6144], F32))
        XT = [E(nc.sbuf_tensor(f"sb_XT{i}", [128, C], F32)) for i in range(2)]
        TMP = E(nc.sbuf_tensor("sb_TMP", [128, C], F32))
        XQBF = E(nc.sbuf_tensor("sb_XQBF", [128, C], BF16))
        GARB = E(nc.sbuf_tensor("sb_GARB", [128, 1024], F32))
        QF16 = E(nc.sbuf_tensor("sb_QF16", [128, C], mybir.dt.float16))
        MASKSB = E(nc.sbuf_tensor("sb_MASKSB", [128, 4 * 512], BF16))
        R1REP = E(nc.sbuf_tensor("sb_R1REP", [128, T], F32))
        SCR = E(nc.sbuf_tensor("sb_SCR", [128, 1024], F32))
        identf = E(nc.sbuf_tensor("sb_identf", [128, 128], F32))
        identb = E(nc.sbuf_tensor("sb_identb", [128, 128], BF16))
        consts = E(nc.sbuf_tensor("sb_consts", [128, NCONST], F32))
        onesf = E(nc.sbuf_tensor("sb_onesf", [128, 2], F32))
        onesb = E(nc.sbuf_tensor("sb_onesb", [128, 1], BF16))
        ones1 = E(nc.sbuf_tensor("sb_ones1", [1, 128], F32))
        stats = E(nc.sbuf_tensor("sb_stats", [128, 160], F32))
        wabs = E(nc.sbuf_tensor("sb_wabs", [128, 96], F32))
        wabs2 = E(nc.sbuf_tensor("sb_wabs2", [128, 32], F32))
        r1c = E(nc.sbuf_tensor("sb_r1c", [128, NTOK], F32))
        r1T = E(nc.sbuf_tensor("sb_r1T", [NTOK, 128], F32))

        sc1 = E(nc.sbuf_tensor("sb_sc1", [1, 32], F32))
        bc = E(nc.sbuf_tensor("sb_bc", [128, 12], F32))
        r1row = R1REP[0:1, :]
        BC_MEANQ, BC_SWQ, BC_MEANP, BC_SWP, BC_NEGM, BC_THP, BC_THN, BC_THP2, BC_THN2 = 0, 1, 2, 3, 4, 5, 6, 7, 8

        pb = [E(nc.psum_tensor(f"ps_pb{i}", [128, 512], F32)) for i in range(8)]

        def f32v(t, c0, n):
            return t[:, c0:c0 + 2 * n].bitcast(F32)

        WB = [f32v(A2, 0, 6144), f32v(A2, 12288, 6144)]
        Qh = f32v(A2, 0, T)
        Kh = f32v(A2, 4096, T)
        Vaug = f32v(A2, 8192, NTOK * 129)
        ERING = [f32v(A2, 12320 + 1024 * i, 512) for i in range(8)]
        TERN = A2[:, 20512:26656]
        OST = [f32v(A2, 26656 + 256 * i, 128) for i in range(4)]
        Q2B = A2[:, 28672:30720]
        K2B = A2[:, 30720:32768]
        AO3 = f32v(A2, 0, C)
        X2S = f32v(A2, 4096, C)

        S_XSUM, S_XSQ, S_NEGM, S_TV, S_TM2, S_VAR, S_R = 0, 16, 32, 48, 64, 80, 96
        S_SD = 112
        S_MX, S_T, S_T2, S_SR, S_C1, S_CC, S_RS = 144, 145, 146, 147, 148, 149, 150

        block = E(nc.Block())

        def p0(p: P):
            y = p.e if p.on("sync") else None
            g = p.e if p.on("gpsimd") else None
            t = p.e if p.on("tensor") else None
            v = p.e if p.on("vector") else None
            s = p.e if p.on("scalar") else None

            if s:
                s.preload_act_table(AF.Sqrt)

            for ap_sb, ap_d in ((identf, identf_d), (identb, identb_d), (MASKSB, masks_d),
                                (consts, consts_d), (onesf, ones_d)):
                p.inc("pre", y.dma_start(out=ap_sb[:], in_=ap_d[:]) if y else None, 16)

            for i in range(NTOK):
                if y and i >= 2:
                    y.wait_ge(sems["chunk"], i - 1)
                p.inc(f"ldx{i % 2}", y.dma_start(out=XT[i % 2][:], in_=x_d[128 * i:128 * (i + 1), :])
                      if y else None, 16)

            for j in range(NCH):
                if g and j >= 2:
                    g.wait_ge(sems["wabs"], 6 * (j - 1))
                p.inc(f"ldw{j % 2}", g.dma_start(out=WB[j % 2], in_=wqkvT_d[j, :, :]) if g else None, 16)

            p.wait("vector", "pre")
            if v:
                v.tensor_copy(onesb[:], onesf[:, 0:1])
                v.memset(ones1[:], 1.0)

            for i in range(NTOK):
                p.wait("scalar", f"ldx{i % 2}", 16 * (i // 2 + 1))
                ins = None
                if s:
                    s.activation(GARB[:, 0:1024], XT[i % 2][:, 0:1024], AF.Square,
                                 accum_out=stats[:, S_XSQ + i:S_XSQ + i + 1])
                    s.drain()
                    ins = s.activation(GARB[:, 0:1024], XT[i % 2][:, 1024:2048], AF.Square,
                                       accum_out=stats[:, S_SD + i:S_SD + i + 1])
                    s.drain()
                p.inc("xsq", ins)
                p.wait("scalar", "var", i + 1)
                p.inc("rsq", s.activation(stats[:, S_SD + i:S_SD + i + 1],
                                          stats[:, S_VAR + i:S_VAR + i + 1], AF.Sqrt) if s else None)
            for j in range(NCH):
                p.wait("scalar", f"ldw{j % 2}", 16 * (j // 2 + 1))
                for m in range(6):
                    col = 6 * j + m
                    p.inc("wabs", s.activation(GARB[:, 0:1024],
                                               WB[j % 2][:, 1024 * m:1024 * (m + 1)],
                                               AF.Abs, accum_out=wabs[:, col:col + 1])
                          if s else None)
                    if s:
                        s.drain()

            for i in range(NTOK):
                p.wait("vector", f"ldx{i % 2}", 16 * (i // 2 + 1))
                if v:
                    v.reduce_sum(stats[:, S_XSUM + i:S_XSUM + i + 1], XT[i % 2][:], axis=AX.X)
                    v.drain()
                    v.tensor_scalar(stats[:, S_NEGM + i:S_NEGM + i + 1],
                                    stats[:, S_XSUM + i:S_XSUM + i + 1], -1.0 / C, 0.0, ALU.mult, ALU.add)
                p.wait("vector", "xsq", i + 1)
                if v:
                    v.tensor_add(stats[:, S_TV + i:S_TV + i + 1],
                                 stats[:, S_XSQ + i:S_XSQ + i + 1],
                                 stats[:, S_SD + i:S_SD + i + 1])
                    v.drain()
                    v.tensor_scalar(stats[:, S_TV + i:S_TV + i + 1],
                                    stats[:, S_TV + i:S_TV + i + 1], 1.0 / C, 1e-5, ALU.mult, ALU.add)
                    v.drain()
                    v.tensor_mul(stats[:, S_TM2 + i:S_TM2 + i + 1],
                                 stats[:, S_NEGM + i:S_NEGM + i + 1],
                                 stats[:, S_NEGM + i:S_NEGM + i + 1])
                    v.drain()
                p.inc("var", v.tensor_sub(stats[:, S_VAR + i:S_VAR + i + 1],
                                          stats[:, S_TV + i:S_TV + i + 1],
                                          stats[:, S_TM2 + i:S_TM2 + i + 1]) if v else None)
                if v:
                    v.tensor_scalar(TMP[:], XT[i % 2][:],
                                    stats[:, S_NEGM + i:S_NEGM + i + 1], 0.0, ALU.add, ALU.add)
                    v.drain()
                p.inc("chunk", v.reduce_max(stats[:, S_MX:S_MX + 1], TMP[:], axis=AX.X,
                                            apply_absolute_value=True) if v else None)
                p.wait("vector", "rsq", i + 1)
                if v:
                    v.reciprocal(stats[:, S_R + i:S_R + i + 1], stats[:, S_SD + i:S_SD + i + 1])
                    v.drain()
                    v.tensor_mul(stats[:, S_T:S_T + 1], stats[:, S_MX:S_MX + 1],
                                 stats[:, S_R + i:S_R + i + 1])
                    v.drain()
                    v.tensor_scalar(stats[:, S_T2:S_T2 + 1], stats[:, S_T:S_T + 1],
                                    1e-5, 0.0, ALU.max, ALU.add)
                    v.drain()
                    v.reciprocal(stats[:, S_SR:S_SR + 1], stats[:, S_T2:S_T2 + 1])
                    v.drain()
                    v.tensor_mul(stats[:, S_C1:S_C1 + 1], stats[:, S_SR:S_SR + 1],
                                 stats[:, S_R + i:S_R + i + 1])
                    v.tensor_scalar(r1c[:, i:i + 1], stats[:, S_T2:S_T2 + 1],
                                    1.0 / 127.0, 0.0, ALU.mult, ALU.add)
                    v.drain()
                    v.tensor_scalar(stats[:, S_CC:S_CC + 1], stats[:, S_C1:S_C1 + 1],
                                    127.0, 0.0, ALU.mult, ALU.add)
                    v.drain()
                    v.tensor_scalar(TMP[:], TMP[:], stats[:, S_CC:S_CC + 1],
                                    127.49999237060547, ALU.mult, ALU.min)
                    v.drain()
                    v.tensor_scalar(QF16[:], TMP[:], -128.5, 1536.0, ALU.max, ALU.add)
                    v.drain()
                p.inc("xqbf", v.tensor_scalar(XQBF[:], QF16[:], -1536.0, 0.0,
                                              ALU.add, ALU.add) if v else None)

                for grp in range(4):
                    gidx = 4 * i + grp
                    p.wait("tensor", "xqbf", i + 1)
                    if gidx >= 2:
                        p.wait("tensor", "evac", gidx - 1)
                    ins = None
                    if t:
                        for jj in range(4):
                            j = grp * 4 + jj
                            ins = t.transpose(pb[gidx % 2][:, 64 * jj:64 * (jj + 1)].bitcast(BF16),
                                              XQBF[:, 128 * j:128 * (j + 1)], identb[:])
                    p.inc("tpose", ins)
                    p.wait("vector", "tpose", gidx + 1)
                    ins = None
                    if v:
                        for jj in range(4):
                            j = grp * 4 + jj
                            ins = v.tensor_copy(A1[:, T * j + 128 * i: T * j + 128 * (i + 1)],
                                                pb[gidx % 2][:, 64 * jj:64 * (jj + 1)].bitcast(BF16))
                    p.inc("evac", ins)

            p.wait("tensor", "xqbf", NTOK)
            p.wait("tensor", "evac", 4 * NTOK)
            p.inc("tpose", t.transpose(pb[2][0:NTOK, 0:128], r1c[:], identf[:]) if t else None)
            p.wait("vector", "tpose", p.cnt["tpose"])
            p.inc("r1T", v.tensor_copy(r1T[:], pb[2][0:NTOK, 0:128]) if v else None)
            p.wait("gpsimd", "r1T")
            p.inc("gb", g.dma_start(out=r1_hbm[:], in_=r1T[:]) if g else None, 16)
            p.wait("gpsimd", "gb")
            p.inc("gb", g.dma_start(out=r1row[:], in_=bass.AP(r1_hbm, 0, [[T, 1], [1, T]]))
                  if g else None, 16)

            p.wait("vector", "wabs", 6 * NCH)
            p.inc("wsum", v.reduce_sum(stats[:, S_MX:S_MX + 1], wabs[:], axis=AX.X) if v else None)
            p.wait("tensor", "wsum")
            p.inc("mmini", t.matmul(pb[3][0:1, 0:1], stats[:, S_MX:S_MX + 1], onesf[:, 0:1],
                                    start=True, stop=True) if t else None)
            p.wait("vector", "mmini")
            if v:
                v.tensor_scalar(sc1[0:1, 0:1], pb[3][0:1, 0:1], 1.0 / (3 * C * C), 0.0, ALU.mult, ALU.add)
                v.drain()
                v.tensor_scalar(sc1[0:1, 1:2], sc1[0:1, 0:1], 1e-5, 0.0, ALU.max, ALU.add)
                v.drain()
            p.inc("wsum", v.reciprocal(sc1[0:1, 2:3], sc1[0:1, 1:2]) if v else None)
            p.wait("tensor", "wsum")
            ins = None
            if t:
                t.matmul(pb[3][:, 0:1], ones1[:], sc1[0:1, 1:2], start=True, stop=True)
                ins = t.matmul(pb[3][:, 4:5], ones1[:], sc1[0:1, 2:3], start=True, stop=True)
            p.inc("mmini", ins)
            p.wait("vector", "mmini")
            ins = None
            if v:
                v.tensor_copy(bc[:, BC_MEANQ:BC_MEANQ + 1], pb[3][:, 0:1])
                v.tensor_copy(bc[:, BC_SWQ:BC_SWQ + 1], pb[3][:, 4:5])
                v.drain()
                v.tensor_scalar(bc[:, BC_THP:BC_THP + 1], bc[:, BC_MEANQ:BC_MEANQ + 1],
                                0.5, 0.0, ALU.mult, ALU.add)
                ins = v.tensor_scalar(bc[:, BC_THN:BC_THN + 1], bc[:, BC_MEANQ:BC_MEANQ + 1],
                                      -0.5, 0.0, ALU.mult, ALU.add)
            p.inc("evac", ins)

            p.wait("vector", "gb")
            p.wait("vector", "evac")
            p.inc("r1T", v.tensor_scalar(r1row[:], r1row[:],
                                         bc[0:1, BC_MEANQ:BC_MEANQ + 1],
                                         0.0, ALU.mult, ALU.add) if v else None)
            p.wait("tensor", "r1T")
            ins = None
            if t:
                for k in range(4):
                    ins = t.matmul(pb[4 + k][:, 0:512], ones1[:],
                                   r1row[0:1, 512 * k:512 * (k + 1)], start=True, stop=True)
            p.inc("mmini", ins)
            p.wait("vector", "mmini")
            ins = None
            if v:
                for k in range(4):
                    ins = v.tensor_copy(R1REP[:, 512 * k:512 * (k + 1)], pb[4 + k][:, 0:512])
            p.inc("evac", ins)

        def p12(p: P):
            y = p.e if p.on("sync") else None
            g = p.e if p.on("gpsimd") else None
            t = p.e if p.on("tensor") else None
            v = p.e if p.on("vector") else None
            s = p.e if p.on("scalar") else None

            if s:
                s.preload_act_table(AF.Exp)

            ntile = 0
            exp_after = {}
            ste_after = {}
            ndiag = 0
            ngq = 0
            nost = 0

            def stream_head_w(h):
                for j in range(NCH):
                    if y:
                        if h >= 1 and j == 0:
                            y.wait_ge(sems["terk"], h)
                        src = bass.AP(wqkvT_d, j * 128 * (3 * C) + 128 * h,
                                      [[3 * C, 128], [C, 3], [1, 128]])
                        dst = bass.AP(WSLAB, 128 * j, [[6144, 128], [C, 3], [1, 128]])
                        ins = y.dma_start(out=dst, in_=src)
                    else:
                        ins = None
                    p.inc("ldh", ins, 16)

            def ternarize_head(h):
                p.wait("vector", "ldh", 16 * NCH * (h + 1))
                if h >= 1:
                    p.wait("vector", "qkvg", 12 * h)
                ins = None
                if v:
                    for m in range(3):
                        v.tensor_scalar(XQBF[:], WSLAB[:, 2048 * m:2048 * (m + 1)],
                                        bc[:, BC_THN:BC_THN + 1], 0.0, ALU.is_lt, ALU.add)
                        v.drain()
                        ins = v.scalar_tensor_tensor(TERN[:, 2048 * m:2048 * (m + 1)],
                                                     WSLAB[:, 2048 * m:2048 * (m + 1)],
                                                     bc[:, BC_THP:BC_THP + 1], XQBF[:],
                                                     ALU.is_gt, ALU.subtract)
                        v.drain()
                p.inc("terk", ins)

            stream_head_w(0)
            ternarize_head(0)
            for h in range(H):
                if h + 1 < H:
                    stream_head_w(h + 1)

                if h < NCH:
                    if g and h >= 2:
                        g.wait_ge(sems["pabs"], 2 * (h - 1))
                    p.inc(f"ldp{h % 2}", g.dma_start(out=XT[h % 2][:], in_=wprojT_d[h, :, :])
                          if g else None, 16)
                    p.wait("scalar", f"ldp{h % 2}", 16 * (h // 2 + 1))
                    for m in range(2):
                        p.inc("pabs", s.activation(GARB[:, 0:1024],
                                                   XT[h % 2][:, 1024 * m:1024 * (m + 1)], AF.Abs,
                                                   accum_out=wabs2[:, 2 * h + m:2 * h + m + 1])
                              if s else None)
                        if s:
                            s.drain()

                p.wait("tensor", "terk", h + 1)
                dest = {0: Qh, 1: Kh}
                for m in range(2):
                    for gq in range(4):
                        grp = 12 * h + m * 4 + gq
                        bank = pb[grp % 2]
                        if m == 0 and gq == 0:
                            p.wait("tensor", "ave", max(0, 16 * h - 1))
                        g_in = m * 4 + gq
                        if h > 0 or g_in >= 2:
                            p.wait("tensor", "qkve", 14 * h + g_in - 1)
                        ins = None
                        if t:
                            for j in range(NCH):
                                ins = t.matmul(bank[:, 0:512],
                                               TERN[:, 2048 * m + 128 * j:2048 * m + 128 * (j + 1)],
                                               A1[:, T * j + 512 * gq: T * j + 512 * (gq + 1)],
                                               start=(j == 0), stop=(j == NCH - 1))
                        p.inc("qkvg", ins)
                        p.wait("vector", "qkvg", grp + 1)
                        ins = None
                        if v:
                            dst = dest[m][:, 512 * gq:512 * (gq + 1)] if m < 2 else \
                                TMP[:, 512 * gq:512 * (gq + 1)]
                            ins = v.tensor_mul(dst, bank[:, 0:512], R1REP[:, 512 * gq:512 * (gq + 1)])
                        p.inc("qkve", ins)

                if v:
                    v.drain()
                p.inc("qkve", v.tensor_mul(Q2B[:], Qh, Qh) if v else None)
                p.inc("qkve", v.tensor_mul(K2B[:], Kh, Kh) if v else None)
                p.wait("tensor", "exp", p.cnt["exp"])
                p.wait("tensor", "qkve", 14 * h + 10)
                p.wait("tensor", "ave", 16 * h)
                p.wait("tensor", "negme", h)
                p.wait("tensor", "vte", 5 * h)
                for k in range(8):
                    srcb = Q2B if k < 4 else K2B
                    p.inc("nrm2", t.matmul(pb[k][0:1, 0:512], onesb[:],
                                           srcb[:, 512 * (k % 4):512 * (k % 4 + 1)],
                                           start=True, stop=True) if t else None)
                for k in range(8):
                    p.wait("vector", "nrm2", 9 * h + k + 1)
                    p.inc("nrme", v.reduce_max(sc1[0:1, 8 + k:9 + k],
                                               pb[k][0:1, 0:512], axis=AX.X) if v else None)
                p.wait("vector", "nrme", 9 * h + 8)
                if v:
                    v.reduce_max(sc1[0:1, 16:17], sc1[0:1, 8:12], axis=AX.X)
                    v.reduce_max(sc1[0:1, 17:18], sc1[0:1, 12:16], axis=AX.X)
                    v.drain()
                    v.tensor_add(sc1[0:1, 18:19], sc1[0:1, 16:17], sc1[0:1, 17:18])
                    v.drain()
                p.inc("nrme", v.tensor_scalar(sc1[0:1, 19:20], sc1[0:1, 18:19],
                                              -0.5 * SCALE, 0.0, ALU.mult, ALU.add) if v else None)
                ins = None
                if v:
                    for tt in range(NTOK):
                        ins = v.memset(Vaug[:, 129 * tt + 128:129 * tt + 129], 1.0)
                p.inc("vte", ins)

                for gq in range(4):
                    grp = 12 * h + 8 + gq
                    bank = pb[grp % 2]
                    if gq < 2:
                        p.wait("tensor", "nrme", 9 * h + gq + 1)
                    p.wait("tensor", "qkve", 14 * h + [7, 8, 11, 12][gq])
                    ins = None
                    if t:
                        for j in range(NCH):
                            ins = t.matmul(bank[:, 0:512],
                                           TERN[:, 4096 + 128 * j:4096 + 128 * (j + 1)],
                                           A1[:, T * j + 512 * gq: T * j + 512 * (gq + 1)],
                                           start=(j == 0), stop=(j == NCH - 1))
                    p.inc("qkvg", ins)
                    p.wait("vector", "qkvg", grp + 1)
                    p.inc("qkve", v.tensor_mul(TMP[:, 512 * gq:512 * (gq + 1)], bank[:, 0:512],
                                               R1REP[:, 512 * gq:512 * (gq + 1)]) if v else None)

                p.wait("tensor", "nrme", p.cnt["nrme"])
                p.inc("nrm2", t.matmul(pb[7][:, 256:257], ones1[:], sc1[0:1, 19:20],
                                       start=True, stop=True) if t else None)
                p.wait("vector", "nrm2", p.cnt["nrm2"])
                p.inc("negme", v.tensor_copy(bc[:, BC_NEGM:BC_NEGM + 1], pb[7][:, 256:257])
                      if v else None)

                for grp in range(4):
                    p.wait("tensor", "qkve", 14 * h + 11 + grp)
                    if grp == 0:
                        p.wait("tensor", "nrme", 9 * h + 3)
                    if grp >= 1:
                        p.wait("tensor", "vte", 5 * h + 1 + grp)
                    ins = None
                    if t:
                        for jj in range(4):
                            tt = grp * 4 + jj
                            ins = t.transpose(pb[2][:, 128 * jj:128 * (jj + 1)],
                                              TMP[:, 128 * tt:128 * (tt + 1)], identf[:])
                            ins.annotate(f"vtp h={h} grp={grp} jj={jj}")
                    p.inc("vtp", ins)
                    p.wait("vector", "vtp", 4 * h + grp + 1)
                    ins = None
                    if v:
                        for jj in range(4):
                            tt = grp * 4 + jj
                            ins = v.tensor_copy(Vaug[:, 129 * tt:129 * tt + 128],
                                                pb[2][:, 128 * jj:128 * (jj + 1)])
                            ins.annotate(f"vte h={h} grp={grp} jj={jj}")
                    p.inc("vte", ins)

                if h + 1 < H:
                    ternarize_head(h + 1)

                p.wait("tensor", "vte", p.cnt["vte"])
                head_base = ntile
                for gq in range(4):
                    nkt = 4 * gq + 4

                    def emit_qk(kt, tile):
                        diag = kt >= 4 * gq
                        qkbank = pb[3 + tile % 4]
                        if tile - head_base < 4:
                            p.wait("tensor", "nrme", 9 * h + 4 + tile % 4)
                        if tile >= 4:
                            p.wait("tensor", "exp", exp_after[tile - 4])
                        ins = t.matmul(qkbank[:, 0:512], Kh[:, 128 * kt:128 * (kt + 1)],
                                       Qh[:, 512 * gq:512 * (gq + 1)], start=True, stop=True) \
                            if t else None
                        p.inc("st_", ins)
                        p.wait("scalar", "st_", p.cnt["st_"])
                        p.wait("scalar", "negme", h + 1)
                        if tile >= 8:
                            p.wait("scalar", "av", tile - 7)
                        p.inc("exp", s.activation(ERING[tile % 8], pb[3 + tile % 4][:, 0:512],
                                                  AF.Exp,
                                                  bias=bc[:, BC_NEGM:BC_NEGM + 1],
                                                  scale=consts[:, CON_SCALE:CON_SCALE + 1])
                              if s else None)
                        exp_after[tile] = p.cnt["exp"]
                        if diag:
                            p.wait("vector", "exp", exp_after[tile])
                            p.inc("ste", v.tensor_mul(ERING[tile % 8], ERING[tile % 8],
                                                      MASKSB[:, 512 * (kt - 4 * gq):512 * (kt - 4 * gq + 1)])
                                  if v else None)
                            ste_after[tile] = p.cnt["ste"]

                    for w0 in range(min(3, nkt)):
                        emit_qk(w0, ntile + w0)
                    for kt in range(nkt):
                        tile = ntile
                        ntile += 1
                        if kt + 3 < nkt:
                            emit_qk(kt + 3, tile + 3)
                        p.wait("tensor", "exp", exp_after[tile])
                        if tile in ste_after:
                            p.wait("tensor", "ste", ste_after[tile])
                        ins = None
                        if t:
                            for sub in range(4):
                                if kt == 0 and ngq > 0:
                                    p.wait("tensor", "ave", 4 * ngq - 3 + sub)
                                obank = pb[[0, 1, 2, 7][sub]]
                                ins = t.matmul(obank[:, 0:129],
                                               ERING[tile % 8][:, 128 * sub:128 * (sub + 1)],
                                               Vaug[:, 129 * kt:129 * (kt + 1)],
                                               start=(kt == 0), stop=(kt == nkt - 1))
                        p.inc("av", ins)
                    ngq += 1
                    p.wait("vector", "av", p.cnt["av"])
                    for sub in range(4):
                        obank, ocol = pb[[0, 1, 2, 7][sub]], 0
                        ost_i = nost
                        nost += 1
                        if ost_i >= 4:
                            p.wait("vector", f"ostd{ost_i % 4}", 16 * (ost_i // 4))
                        ins = None
                        if v:
                            v.reciprocal(stats[:, S_RS + sub:S_RS + sub + 1],
                                         obank[:, ocol + 128:ocol + 129])
                            v.drain()
                            ins = v.tensor_scalar(OST[ost_i % 4], obank[:, ocol:ocol + 128],
                                                  stats[:, S_RS + sub:S_RS + sub + 1],
                                                  0.0, ALU.mult, ALU.add)
                        p.inc("ave", ins, 1)
                        p.wait("gpsimd", "ave", p.cnt["ave"])
                        q0 = 512 * gq + 128 * sub
                        p.inc(f"ostd{ost_i % 4}", g.dma_start(out=o_hbm[h, q0:q0 + 128, :],
                                                              in_=OST[ost_i % 4])
                              if g else None, 16)

            if "QK" in dbg:
                p.wait("gpsimd", "qkve")
                p.inc("dbgs", g.dma_start(out=dbg["QK"][:, 0:T], in_=Qh) if g else None, 16)
                p.inc("dbgs", g.dma_start(out=dbg["QK"][:, T:2 * T], in_=Kh) if g else None, 16)
            if "TERN" in dbg:
                p.wait("gpsimd", "terk")
                p.inc("dbgs", g.dma_start(out=dbg["TERN"][:], in_=TERN[:]) if g else None, 16)
            if "Vaug" in dbg:
                p.wait("gpsimd", "vte")
                p.inc("dbgs", g.dma_start(out=dbg["Vaug"][:], in_=Vaug) if g else None, 16)

        def p3(p: P):
            y = p.e if p.on("sync") else None
            g = p.e if p.on("gpsimd") else None
            t = p.e if p.on("tensor") else None
            v = p.e if p.on("vector") else None
            s = p.e if p.on("scalar") else None

            if s:
                s.preload_act_table(AF.Sqrt)

            for q in range(4):
                p.wait("gpsimd", f"ostd{q}")

            p.wait("vector", "pabs", 32)
            p.inc("vs3", v.reduce_sum(stats[:, S_MX:S_MX + 1], wabs2[:], axis=AX.X) if v else None)
            p.wait("tensor", "vs3")
            p.wait("tensor", "negme", H)
            p.inc("mmini", t.matmul(pb[7][0:1, 0:1], stats[:, S_MX:S_MX + 1], onesf[:, 0:1],
                                    start=True, stop=True) if t else None)
            p.wait("vector", "mmini")
            if v:
                v.tensor_scalar(sc1[0:1, 0:1], pb[7][0:1, 0:1], 1.0 / (C * C), 0.0, ALU.mult, ALU.add)
                v.drain()
            p.inc("vs3", v.tensor_scalar(sc1[0:1, 1:2], sc1[0:1, 0:1],
                                         1e-5, 0.0, ALU.max, ALU.add) if v else None)
            p.wait("tensor", "vs3")
            p.inc("mmini", t.matmul(pb[7][:, 0:1], ones1[:], sc1[0:1, 1:2],
                                    start=True, stop=True) if t else None)
            p.wait("vector", "mmini")
            ins = None
            if v:
                v.tensor_copy(bc[:, BC_MEANP:BC_MEANP + 1], pb[7][:, 0:1])
                v.drain()
                v.tensor_scalar(bc[:, BC_THP2:BC_THP2 + 1], bc[:, BC_MEANP:BC_MEANP + 1],
                                0.5, 0.0, ALU.mult, ALU.add)
                ins = v.tensor_scalar(bc[:, BC_THN2:BC_THN2 + 1], bc[:, BC_MEANP:BC_MEANP + 1],
                                      -0.5, 0.0, ALU.mult, ALU.add)
            p.inc("vs3", ins)

            for j in range(NCH):
                if y:
                    if j == 0:
                        y.wait_ge(sems["terk"], H)
                    if j >= 3:
                        y.wait_ge(sems["ternp"], 2 * (j - 2))
                p.inc(f"ldq{j % 3}", y.dma_start(out=WSLAB[:, 2048 * (j % 3):2048 * (j % 3 + 1)],
                                                 in_=wprojT_d[j, :, :]) if y else None, 16)
            for j in range(NCH):
                p.wait("vector", f"ldq{j % 3}", 16 * (j // 3 + 1))
                if j == 0:
                    p.wait("vector", "qkvg", 12 * H)
                    p.wait("vector", "nrme", 9 * H)
                    p.wait("vector", "vs3", p.cnt["vs3"])
                ins = None
                if v:
                    v.tensor_scalar(Q2B[:], WSLAB[:, 2048 * (j % 3):2048 * (j % 3 + 1)],
                                    bc[:, BC_THN2:BC_THN2 + 1], 0.0, ALU.is_lt, ALU.add)
                    v.drain()
                    ins = v.scalar_tensor_tensor(A1[:, T * j:T * j + 2048],
                                                 WSLAB[:, 2048 * (j % 3):2048 * (j % 3 + 1)],
                                                 bc[:, BC_THP2:BC_THP2 + 1], Q2B[:],
                                                 ALU.is_gt, ALU.subtract)
                    v.drain()
                p.inc("ternp", ins, 2)

            S3_M, S3_T2, S3_SR, S3_C, S3_R2M0, S3_R2M1, S3_NM, S3_TV, S3_M2, S3_VAR, S3_SD, S3_RN = \
                range(128, 140)
            AO3P = [f32v(A2, 0, C), f32v(A2, 4096, C)]
            X2S3 = f32v(A2, 8192, C)
            XRES = f32v(A2, 12288, C)
            OTR = [f32v(A2, 16384, C), f32v(A2, 20480, C), f32v(A2, 24576, C)]

            def p3_ln2x2(ic):
                p.wait("scalar", "proje", 4 * (ic + 1))
                p.wait("scalar", "x2st", ic)
                AO = AO3P[ic % 2]
                ins = None
                if s:
                    s.activation(GARB[:, 0:1024], AO[:, 0:1024], AF.Square,
                                 accum_out=stats[:, S3_M2:S3_M2 + 1])
                    s.drain()
                    ins = s.activation(GARB[:, 0:1024], AO[:, 1024:2048], AF.Square,
                                       accum_out=stats[:, S3_SD:S3_SD + 1])
                    s.drain()
                p.inc("sv3", ins)
                if v:
                    v.drain()
                    v.reduce_sum(stats[:, S3_NM:S3_NM + 1], AO[:], axis=AX.X)
                    v.drain()
                    v.tensor_scalar(stats[:, S3_NM:S3_NM + 1], stats[:, S3_NM:S3_NM + 1],
                                    -1.0 / C, 0.0, ALU.mult, ALU.add)
                    v.drain()
                p.wait("vector", "sv3", p.cnt["sv3"])
                if v:
                    v.tensor_add(stats[:, S3_TV:S3_TV + 1], stats[:, S3_M2:S3_M2 + 1],
                                 stats[:, S3_SD:S3_SD + 1])
                    v.drain()
                    v.tensor_scalar(stats[:, S3_TV:S3_TV + 1], stats[:, S3_TV:S3_TV + 1],
                                    1.0 / C, 1e-5, ALU.mult, ALU.add)
                    v.drain()
                    v.tensor_mul(stats[:, S3_M2:S3_M2 + 1], stats[:, S3_NM:S3_NM + 1],
                                 stats[:, S3_NM:S3_NM + 1])
                    v.drain()
                p.inc("vs3", v.tensor_sub(stats[:, S3_VAR:S3_VAR + 1],
                                          stats[:, S3_TV:S3_TV + 1],
                                          stats[:, S3_M2:S3_M2 + 1]) if v else None)
                p.wait("scalar", "vs3", p.cnt["vs3"])
                p.inc("sv3", s.activation(stats[:, S3_SD:S3_SD + 1],
                                          stats[:, S3_VAR:S3_VAR + 1], AF.Sqrt) if s else None)
                if y and ic >= 1:
                    y.wait_ge(sems["x2st"], ic)
                p.inc(f"ldx{ic % 2}", y.dma_start(out=XRES, in_=x_d[128 * ic:128 * (ic + 1), :])
                      if y else None, 16)
                ldx_tgt = p.cnt[f"ldx{ic % 2}"]
                p.wait("vector", "sv3", p.cnt["sv3"])
                if v:
                    v.reciprocal(stats[:, S3_RN:S3_RN + 1], stats[:, S3_SD:S3_SD + 1])
                    v.drain()
                    v.tensor_scalar(TMP[:], AO[:], stats[:, S3_NM:S3_NM + 1],
                                    stats[:, S3_RN:S3_RN + 1], ALU.add, ALU.mult)
                    v.drain()
                if ic >= 1:
                    p.wait("vector", f"ostd{(ic - 1) % 2}", p.cnt[f"ostd{(ic - 1) % 2}"])
                p.wait("vector", f"ldx{ic % 2}", ldx_tgt)
                p.inc("x2st", v.tensor_add(X2S3, TMP[:], XRES) if v else None)
                p.wait("gpsimd", "x2st", p.cnt["x2st"])
                p.inc(f"ostd{ic % 2}", g.dma_start(out=x2_hbm[128 * ic:128 * (ic + 1), :],
                                                   in_=X2S3) if g else None, 16)

            def p3_gather(i):
                if g and i >= 3:
                    g.wait_ge(sems["oq"], i - 2)
                for hh2 in range(2):
                    ins = None
                    if g:
                        srcg = bass.AP(o_hbm, (8 * hh2) * T * D + 128 * i * D,
                                       [[D, 128], [T * D, 8], [1, D]])
                        dstg = OTR[i % 3][:, 1024 * hh2:1024 * (hh2 + 1)].rearrange(
                            "p (h d) -> p h d", d=D)
                        ins = g.dma_start(out=dstg, in_=srcg)
                    p.inc(f"ldo{i % 3}", ins, 16)

            def p3_quant(i):
                p.wait("vector", f"ldo{i % 3}", 32 * (i // 3 + 1))
                if i == 0:
                    p.wait("vector", "ternp", 2 * NCH)
                r2m = S3_R2M0 if i % 2 == 0 else S3_R2M1
                OT = OTR[i % 3]
                if v:
                    v.reduce_max(stats[:, S3_M:S3_M + 1], OT, axis=AX.X,
                                 apply_absolute_value=True)
                    v.drain()
                    v.tensor_scalar(stats[:, S3_T2:S3_T2 + 1], stats[:, S3_M:S3_M + 1],
                                    1e-5, 0.0, ALU.max, ALU.add)
                    v.drain()
                    v.reciprocal(stats[:, S3_SR:S3_SR + 1], stats[:, S3_T2:S3_T2 + 1])
                    v.drain()
                    v.tensor_scalar(stats[:, S3_C:S3_C + 1], stats[:, S3_SR:S3_SR + 1],
                                    127.0, 0.0, ALU.mult, ALU.add)
                    v.tensor_scalar(stats[:, r2m:r2m + 1], stats[:, S3_T2:S3_T2 + 1],
                                    bc[:, BC_MEANP:BC_MEANP + 1], 1.0 / 127.0, ALU.mult, ALU.mult)
                    v.drain()
                    v.tensor_scalar(TMP[:], OT, stats[:, S3_C:S3_C + 1],
                                    127.49999237060547, ALU.mult, ALU.min)
                    v.drain()
                    v.tensor_scalar(QF16[:], TMP[:], -128.5, 1536.0, ALU.max, ALU.add)
                    v.drain()
                p.inc("oq", v.tensor_scalar(XQBF[:], QF16[:], -1536.0, 0.0, ALU.add, ALU.add)
                      if v else None)

            p3_gather(0)
            p3_gather(1)
            p3_quant(0)
            for i in range(NTOK):
                if i + 2 < NTOK:
                    p3_gather(i + 2)

                for grp in range(4):
                    gidx = 4 * i + grp
                    p.wait("tensor", "oq", i + 1)
                    if gidx < 2:
                        p.wait("tensor", "ave", 16 * H)
                    else:
                        p.wait("tensor", "oqTe", gidx - 1)
                    ins = None
                    if t:
                        for jj in range(4):
                            j = grp * 4 + jj
                            ins = t.transpose(pb[gidx % 2][:, 64 * jj:64 * (jj + 1)].bitcast(BF16),
                                              XQBF[:, 128 * j:128 * (j + 1)], identb[:])
                    p.inc("oqT", ins)
                    p.wait("vector", "oqT", p.cnt["oqT"])
                    ins = None
                    if v:
                        for jj in range(4):
                            j = grp * 4 + jj
                            ins = v.tensor_copy(Q2B[:, 128 * j:128 * (j + 1)],
                                                pb[gidx % 2][:, 64 * jj:64 * (jj + 1)].bitcast(BF16))
                    p.inc("oqTe", ins)

                if i + 1 < NTOK:
                    p3_quant(i + 1)

                for fg in range(4):
                    if i == 0:
                        p.wait("tensor", "ternp", 2 * NCH)
                    if i >= 1:
                        p.wait("tensor", "proje", 4 * (i - 1) + fg + 1)
                    p.wait("tensor", "oqTe", 4 * (i + 1))
                    ins = None
                    if t:
                        for j in range(NCH):
                            ins = t.matmul(pb[2 + fg][:, 0:512], Q2B[:, 128 * j:128 * (j + 1)],
                                           A1[:, T * j + 512 * fg:T * j + 512 * (fg + 1)],
                                           start=(j == 0), stop=(j == NCH - 1))
                    p.inc("projg", ins)
                    p.wait("vector", "projg", p.cnt["projg"])
                    r2mi = S3_R2M0 if i % 2 == 0 else S3_R2M1
                    p.inc("proje", v.tensor_scalar(AO3P[i % 2][:, 512 * fg:512 * (fg + 1)],
                                                   pb[2 + fg][:, 0:512],
                                                   stats[:, r2mi:r2mi + 1], 0.0,
                                                   ALU.mult, ALU.add) if v else None)

                if i >= 1:
                    p3_ln2x2(i - 1)
            p3_ln2x2(NTOK - 1)

        def p4(p: P):
            y = p.e if p.on("sync") else None
            g = p.e if p.on("gpsimd") else None
            t = p.e if p.on("tensor") else None
            v = p.e if p.on("vector") else None
            s = p.e if p.on("scalar") else None

            NS = 16
            FS = 4
            LHI = A1
            LLO = A2
            LO3 = QF16[:].bitcast(BF16)
            GHI = WSLAB[:, 0:4096].bitcast(BF16)
            GLOa = WSLAB[:, 4096:6144].bitcast(BF16)
            GLOb = TMP[:].bitcast(BF16)
            WGHI = XT[0][:, 0:1024].bitcast(BF16)
            WGLO = XT[0][:, 1024:2048].bitcast(BF16)
            WUHI = XT[1][:, 0:1024].bitcast(BF16)
            WULO = XT[1][:, 1024:2048].bitcast(BF16)
            WDBUF = [XQBF[:, 0:1024].bitcast(F32), XQBF[:, 1024:2048].bitcast(F32)]
            WDHP = [MASKSB, WGHI]
            WDLP = [LO3, WGLO]
            GTMP = GARB[:, 0:512]
            STG = [GARB[:, 0:1024], SCR[:, 0:1024]]

            def glo(jc):
                return GLOa[:, 2048 * jc:2048 * (jc + 1)] if jc < 2 else \
                    GLOb[:, 2048 * (jc - 2):2048 * (jc - 1)]

            if s:
                s.preload_act_table(AF.Sqrt)
            p.wait("sync", "ostd0")
            p.wait("sync", "ostd1")

            S4 = dict(NM=128, TV=129, M2=130, VAR=131, SD=132, RN=133)
            for i in range(NTOK):
                if y and i >= 2:
                    y.wait_ge(sems["l3"], 2 * (i - 1))
                p.inc(f"ldx{i % 2}", y.dma_start(out=XT[i % 2][:],
                                                 in_=x2_hbm[128 * i:128 * (i + 1), :])
                      if y else None, 16)
                ldt = p.cnt[f"ldx{i % 2}"]
                p.wait("scalar", f"ldx{i % 2}", ldt)
                p.wait("scalar", "l3", 2 * i)
                ins = None
                if s:
                    s.activation(GARB[:], XT[i % 2][:, 0:1024], AF.Square,
                                 accum_out=stats[:, S4["M2"]:S4["M2"] + 1])
                    s.drain()
                    ins = s.activation(GARB[:], XT[i % 2][:, 1024:2048], AF.Square,
                                       accum_out=stats[:, S4["SD"]:S4["SD"] + 1])
                    s.drain()
                p.inc("sv4", ins)
                p.wait("vector", f"ldx{i % 2}", ldt)
                if v:
                    v.reduce_sum(stats[:, S4["NM"]:S4["NM"] + 1], XT[i % 2][:], axis=AX.X)
                    v.drain()
                    v.tensor_scalar(stats[:, S4["NM"]:S4["NM"] + 1],
                                    stats[:, S4["NM"]:S4["NM"] + 1], -1.0 / C, 0.0,
                                    ALU.mult, ALU.add)
                    v.drain()
                p.wait("vector", "sv4", p.cnt["sv4"])
                if v:
                    v.tensor_add(stats[:, S4["TV"]:S4["TV"] + 1],
                                 stats[:, S4["M2"]:S4["M2"] + 1],
                                 stats[:, S4["SD"]:S4["SD"] + 1])
                    v.drain()
                    v.tensor_scalar(stats[:, S4["TV"]:S4["TV"] + 1],
                                    stats[:, S4["TV"]:S4["TV"] + 1], 1.0 / C, 1e-5,
                                    ALU.mult, ALU.add)
                    v.drain()
                    v.tensor_mul(stats[:, S4["M2"]:S4["M2"] + 1],
                                 stats[:, S4["NM"]:S4["NM"] + 1],
                                 stats[:, S4["NM"]:S4["NM"] + 1])
                    v.drain()
                p.inc("vs4", v.tensor_sub(stats[:, S4["VAR"]:S4["VAR"] + 1],
                                          stats[:, S4["TV"]:S4["TV"] + 1],
                                          stats[:, S4["M2"]:S4["M2"] + 1]) if v else None)
                p.wait("scalar", "vs4", p.cnt["vs4"])
                p.inc("sv4", s.activation(stats[:, S4["SD"]:S4["SD"] + 1],
                                          stats[:, S4["VAR"]:S4["VAR"] + 1], AF.Sqrt)
                      if s else None)
                p.wait("vector", "sv4", p.cnt["sv4"])
                if v:
                    v.reciprocal(stats[:, S4["RN"]:S4["RN"] + 1], stats[:, S4["SD"]:S4["SD"] + 1])
                    v.drain()
                    v.tensor_scalar(TMP[:], XT[i % 2][:], stats[:, S4["NM"]:S4["NM"] + 1],
                                    stats[:, S4["RN"]:S4["RN"] + 1], ALU.add, ALU.mult)
                    v.drain()
                    v.tensor_copy(XQBF[:], TMP[:])
                    v.drain()
                p.inc("l3", v.tensor_tensor(LO3, TMP[:], XQBF[:], ALU.subtract)
                      if v else None, 2)
                for half in range(2):
                    for grp in range(4):
                        gidx = 8 * i + 4 * half + grp
                        p.wait("tensor", "l3", 2 * (i + 1))
                        if gidx >= 2:
                            p.wait("tensor", "l3te", gidx - 1)
                        elif "p3" in phases:
                            p.wait("tensor", "proje", 64)
                        ins = None
                        if t:
                            for jj in range(4):
                                j = grp * 4 + jj
                                sap = XQBF[:, 128 * j:128 * (j + 1)] if half == 0 else \
                                    LO3[:, 128 * j:128 * (j + 1)]
                                ins = t.transpose(pb[gidx % 2][:, 64 * jj:64 * (jj + 1)].bitcast(BF16),
                                                  sap, identb[:])
                        p.inc("l3tp", ins)
                        p.wait("vector", "l3tp", p.cnt["l3tp"])
                        dslab = LHI if half == 0 else LLO
                        ins = None
                        if v:
                            for jj in range(4):
                                j = grp * 4 + jj
                                ins = v.tensor_copy(dslab[:, T * j + 128 * i:T * j + 128 * (i + 1)],
                                                    pb[gidx % 2][:, 64 * jj:64 * (jj + 1)].bitcast(BF16))
                        p.inc("l3te", ins)

            p.wait("scalar", "sv4")
            p.inc("sv4", s.preload_act_table(AF.Silu) if s else None)

            ldcnt = {}

            def emit_wload(ft, which):
                wsrc = wgT_d if which == 0 else wuT_d
                sem_l = "ldwg" if which == 0 else "ldwu"
                if y and 2 * ft + which >= 1:
                    y.wait_ge(sems["wsp"], 4 * ft + 2 * which)
                for qq in range(4):
                    src = bass.AP(wsrc, ft * C * 128 + 128 * 128 * 4 * qq,
                                  [[128, 128], [128 * 128, 4], [1, 128]])
                    dst = bass.AP(R1REP, 128 * 4 * qq, [[2048, 128], [128, 4], [1, 128]])
                    p.inc(sem_l, y.dma_start(out=dst, in_=src) if y else None, 16)
                ldcnt[(ft, which)] = p.cnt[sem_l]

            def emit_wcopy(ft, which):
                sem_l = "ldwg" if which == 0 else "ldwu"
                p.wait("vector", sem_l, ldcnt[(ft, which)])
                if ft > 0:
                    p.wait("vector", "gu", 8 * ft - (1 if which == 0 else 0))
                if which == 0 and ft % FS == 0 and ft > 0:
                    p.wait("vector", "dn", 64 * (ft // FS))
                hi = WGHI if which == 0 else WUHI
                lo = WGLO if which == 0 else WULO
                if v:
                    v.drain()
                p.inc("wsp", v.tensor_copy(hi, R1REP[:]) if v else None)
                if v:
                    v.drain()
                p.inc("wsp", v.tensor_tensor(lo, R1REP[:], hi, ALU.subtract)
                      if v else None)
                if v:
                    v.drain()

            def emit_wpair(ft, which):
                emit_wcopy(ft, which)
                nxt = 2 * ft + which + 1
                if nxt < 2 * NS * FS:
                    emit_wload(nxt // 2, nxt % 2)

            ngu = 0
            ndn = 0
            nwd = 0
            for sp in range(NS):
                for fl in range(FS):
                    ft = FS * sp + fl
                    if ft == 0:
                        emit_wload(0, 0)
                        emit_wpair(0, 0)
                        emit_wpair(0, 1)
                    elif fl == 0:
                        emit_wpair(ft, 0)
                        emit_wpair(ft, 1)
                    for tokg in range(4):
                        for which in range(2):
                            gidx = ngu
                            ngu += 1
                            bank = pb[(2 if which == 0 else 3) + 2 * (tokg % 2)]
                            p.wait("tensor", "wsp", 4 * ft + 2 * which + 1)
                            if fl == 0 and tokg <= 1:
                                p.wait("tensor", "dne", 64 * sp)
                            if gidx >= 4:
                                p.wait("tensor", "gsp", (gidx - 4) // 2 + 1)
                            elif "p3" in phases:
                                p.wait("tensor", "proje", 64)
                            hi = WGHI if which == 0 else WUHI
                            lo = WGLO if which == 0 else WULO
                            ins = None
                            if t:
                                for j in range(NCH):
                                    ins = t.matmul(bank[:, 0:512],
                                                   hi[:, 128 * j:128 * (j + 1)],
                                                   LHI[:, T * j + 512 * tokg:T * j + 512 * (tokg + 1)],
                                                   start=(j == 0), stop=False)
                                for j in range(NCH):
                                    ins = t.matmul(bank[:, 0:512],
                                                   hi[:, 128 * j:128 * (j + 1)],
                                                   LLO[:, T * j + 512 * tokg:T * j + 512 * (tokg + 1)],
                                                   start=False, stop=False)
                            p.wait("tensor", "wsp", 4 * ft + 2 * which + 2)
                            ins = None
                            if t:
                                for j in range(NCH):
                                    ins = t.matmul(bank[:, 0:512],
                                                   lo[:, 128 * j:128 * (j + 1)],
                                                   LHI[:, T * j + 512 * tokg:T * j + 512 * (tokg + 1)],
                                                   start=False, stop=(j == NCH - 1))
                            p.inc("gu", ins)
                        gbank = pb[2 + 2 * (tokg % 2)]
                        ubank = pb[3 + 2 * (tokg % 2)]
                        p.wait("scalar", "gu", p.cnt["gu"])
                        p.wait("scalar", "sv4", p.cnt["sv4"])
                        if fl == 0 and tokg == 0:
                            for _sl in range(2):
                                p.wait("scalar", f"accw{_sl}", p.cnt[f"accw{_sl}"])
                        if ngu // 2 - 1 > 0:
                            p.wait("scalar", "gsp", ngu // 2 - 1)
                        p.inc("sil", s.activation(GTMP, gbank[:, 0:512], AF.Silu)
                              if s else None)
                        p.wait("vector", "sil", p.cnt["sil"])
                        if sp > 0 and fl == 0 and tokg == 0:
                            p.wait("vector", "dn", 64 * sp)
                        ins = None
                        if v:
                            v.tensor_tensor(GTMP, GTMP, ubank[:, 0:512], ALU.mult)
                            v.drain()
                            v.tensor_copy(GHI[:, 2048 * fl + 512 * tokg:2048 * fl + 512 * (tokg + 1)],
                                          GTMP)
                            v.drain()
                            ins = v.tensor_tensor(glo(fl)[:, 512 * tokg:512 * (tokg + 1)],
                                                  GTMP,
                                                  GHI[:, 2048 * fl + 512 * tokg:2048 * fl + 512 * (tokg + 1)],
                                                  ALU.subtract)
                            v.drain()
                        p.inc("gsp", ins)
                        if tokg == 2 and fl < FS - 1:
                            emit_wpair(ft + 1, 0)
                    if fl < FS - 1:
                        emit_wpair(ft + 1, 1)

                for cg in range(4):
                    cg_glob = 4 * sp + cg
                    WDHI = WDHP[cg % 2]
                    WDLO = WDLP[cg % 2]
                    for pair in range(2):
                        for jj in range(2):
                            jc = 2 * pair + jj
                            k = nwd
                            nwd += 1
                            bi = k % 2
                            if y and k >= 2:
                                _j2 = (k - 2) % 4
                                y.wait_ge(sems["wdsp"],
                                          8 * ((k - 2) // 4) + [3, 4, 7, 8][_j2])
                            src = wdT_d[512 * sp + 128 * jc:512 * sp + 128 * (jc + 1),
                                        512 * cg:512 * (cg + 1)]
                            p.inc(f"ldwd{bi}", y.dma_start(out=WDBUF[bi], in_=src)
                                  if y else None, 16)
                            p.wait("vector", f"ldwd{bi}", p.cnt[f"ldwd{bi}"])
                            if jc == 0:
                                p.wait("vector", "dn", max(0, p.cnt["dn"] - 16))
                            if v:
                                v.drain()
                            p.inc("wdsp", v.tensor_copy(WDHI[:, 512 * jc:512 * (jc + 1)],
                                                        WDBUF[bi]) if v else None)
                        for jj in range(2):
                            jc = 2 * pair + jj
                            if v:
                                v.drain()
                            p.inc("wdsp", v.tensor_tensor(WDLO[:, 512 * jc:512 * (jc + 1)],
                                                          WDBUF[jc % 2],
                                                          WDHI[:, 512 * jc:512 * (jc + 1)],
                                                          ALU.subtract) if v else None)
                            if v:
                                v.drain()
                    for tt in range(NTOK):
                        gidx = ndn
                        ndn += 1
                        bank = pb[2 + gidx % 6]
                        base = 8 * cg_glob
                        p.wait("tensor", "gsp", p.cnt["gsp"])
                        if gidx >= 6:
                            p.wait("tensor", "dne", gidx - 5)
                        ins = None
                        WDHIb = WDHP[cg % 2]
                        WDLOb = WDLP[cg % 2]
                        for jc in range(4):
                            if tt == 0:
                                p.wait("tensor", "wdsp",
                                       base + (jc + 1 if jc < 2 else jc + 3))
                            ins = t.matmul(bank[:, 0:512],
                                           GHI[:, 2048 * jc + 128 * tt:2048 * jc + 128 * (tt + 1)],
                                           WDHIb[:, 512 * jc:512 * (jc + 1)],
                                           start=(jc == 0), stop=False) if t else None
                        for jc in range(4):
                            ins = t.matmul(bank[:, 0:512],
                                           glo(jc)[:, 128 * tt:128 * (tt + 1)],
                                           WDHIb[:, 512 * jc:512 * (jc + 1)],
                                           start=False, stop=False) if t else None
                        for jc in range(4):
                            if tt == 0:
                                p.wait("tensor", "wdsp",
                                       base + (jc + 3 if jc < 2 else jc + 5))
                            ins = t.matmul(bank[:, 0:512],
                                           GHI[:, 2048 * jc + 128 * tt:2048 * jc + 128 * (tt + 1)],
                                           WDLOb[:, 512 * jc:512 * (jc + 1)],
                                           start=False, stop=(jc == 3)) if t else None
                        p.inc("dn", ins)
                        pair = (gidx // 2) % 2
                        half = gidx % 2
                        p.wait("scalar", "dn", p.cnt["dn"])
                        if half == 0 and gidx >= 4:
                            p.wait("scalar", f"accw{pair}", p.cnt[f"accw{pair}"])
                        p.inc("dne", s.copy(STG[pair][:, 512 * half:512 * (half + 1)],
                                            bank[:, 0:512]) if s else None)
                        if s:
                            s.drain()
                        if half == 1:
                            p.wait("gpsimd", "dne", p.cnt["dne"])
                            dst = bass.AP(acc_hbm, 128 * (tt - 1) * C + 512 * cg,
                                          [[C, 128], [128 * C, 2], [1, 512]])
                            ins = None
                            if g:
                                if sp == 0:
                                    ins = g.dma_start(out=dst, in_=STG[pair])
                                else:
                                    ins = g.dma_start(out=dst, in_=STG[pair], accum_op=ALU.add)
                            p.inc(f"accw{pair}", ins, 16)

            p.wait("scalar", "sv4")
            p.inc("sv4", s.preload_act_table(AF.Sqrt) if s else None)
            p.wait("sync", "accw0")
            p.wait("sync", "accw1")
            OF = [WSLAB[:, 2048:4096], WSLAB[:, 4096:6144]]
            LN4T = WSLAB[:, 0:2048]
            for i in range(NTOK):
                if y:
                    if i >= 2:
                        y.wait_ge(sems["x2st"], NTOK + i - 1)
                p.inc(f"ldac{i % 2}", y.dma_start(out=XT[i % 2][:],
                                                  in_=acc_hbm[128 * i:128 * (i + 1), :])
                      if y else None, 16)
                if y and i >= 1:
                    y.wait_ge(sems["x2st"], NTOK + i)
                p.inc(f"ldac{i % 2}", y.dma_start(out=TMP[:],
                                                  in_=x2_hbm[128 * i:128 * (i + 1), :])
                      if y else None, 16)
                ldt = p.cnt[f"ldac{i % 2}"]
                p.wait("scalar", f"ldac{i % 2}", ldt)
                p.wait("scalar", "x2st", NTOK + i)
                ins = None
                if s:
                    s.activation(GARB[:], XT[i % 2][:, 0:1024], AF.Square,
                                 accum_out=stats[:, S4["M2"]:S4["M2"] + 1])
                    s.drain()
                    ins = s.activation(GARB[:], XT[i % 2][:, 1024:2048], AF.Square,
                                       accum_out=stats[:, S4["SD"]:S4["SD"] + 1])
                    s.drain()
                p.inc("sv4", ins)
                p.wait("vector", f"ldac{i % 2}", ldt)
                if v:
                    v.reduce_sum(stats[:, S4["NM"]:S4["NM"] + 1], XT[i % 2][:], axis=AX.X)
                    v.drain()
                    v.tensor_scalar(stats[:, S4["NM"]:S4["NM"] + 1],
                                    stats[:, S4["NM"]:S4["NM"] + 1], -1.0 / C, 0.0,
                                    ALU.mult, ALU.add)
                    v.drain()
                p.wait("vector", "sv4", p.cnt["sv4"])
                if v:
                    v.tensor_add(stats[:, S4["TV"]:S4["TV"] + 1],
                                 stats[:, S4["M2"]:S4["M2"] + 1],
                                 stats[:, S4["SD"]:S4["SD"] + 1])
                    v.drain()
                    v.tensor_scalar(stats[:, S4["TV"]:S4["TV"] + 1],
                                    stats[:, S4["TV"]:S4["TV"] + 1], 1.0 / C, 1e-5,
                                    ALU.mult, ALU.add)
                    v.drain()
                    v.tensor_mul(stats[:, S4["M2"]:S4["M2"] + 1],
                                 stats[:, S4["NM"]:S4["NM"] + 1],
                                 stats[:, S4["NM"]:S4["NM"] + 1])
                    v.drain()
                p.inc("vs4", v.tensor_sub(stats[:, S4["VAR"]:S4["VAR"] + 1],
                                          stats[:, S4["TV"]:S4["TV"] + 1],
                                          stats[:, S4["M2"]:S4["M2"] + 1]) if v else None)
                p.wait("scalar", "vs4", p.cnt["vs4"])
                p.inc("sv4", s.activation(stats[:, S4["SD"]:S4["SD"] + 1],
                                          stats[:, S4["VAR"]:S4["VAR"] + 1], AF.Sqrt)
                      if s else None)
                p.wait("vector", "sv4", p.cnt["sv4"])
                if v:
                    v.reciprocal(stats[:, S4["RN"]:S4["RN"] + 1], stats[:, S4["SD"]:S4["SD"] + 1])
                    v.drain()
                    v.tensor_scalar(LN4T, XT[i % 2][:], stats[:, S4["NM"]:S4["NM"] + 1],
                                    stats[:, S4["RN"]:S4["RN"] + 1], ALU.add, ALU.mult)
                    v.drain()
                if i >= 2:
                    p.wait("vector", f"outw{i % 2}", 16 * (i // 2))
                p.inc("x2st", v.tensor_add(OF[i % 2], LN4T, TMP[:]) if v else None)
                p.wait("gpsimd", "x2st", p.cnt["x2st"])
                p.inc(f"outw{i % 2}", g.dma_start(out=out_d[128 * i:128 * (i + 1), :],
                                                  in_=OF[i % 2]) if g else None, 16)
            p.wait("gpsimd", "outw0")
            p.wait("gpsimd", "outw1")

        def emit_all(p: P):
            if "p0" in phases:
                p0(p)
            if "p12" in phases:
                p12(p)
            if "p3" in phases:
                p3(p)
            if "p4" in phases:
                p4(p)
            g = p.e if p.on("gpsimd") else None
            if dbg or True:
                p.wait("gpsimd", "evac")
                p.wait("gpsimd", "xqbf")
                if "xqT" in dbg:
                    p.inc("dbgs", g.dma_start(out=dbg["xqT"][:], in_=A1[:]) if g else None, 16)
                if "r1" in dbg:
                    p.inc("dbgs", g.dma_start(out=dbg["r1"][:], in_=r1c[:]) if g else None, 16)
                if "wmean" in dbg:
                    p.inc("dbgs", g.dma_start(out=dbg["wmean"][:], in_=bc[:, 0:2]) if g else None, 16)
                if "R1rep" in dbg:
                    p.inc("dbgs", g.dma_start(out=dbg["R1rep"][:], in_=R1REP[:]) if g else None, 16)
                p.wait("gpsimd", "dbgs")
                for q in range(4):
                    p.wait("gpsimd", f"ostd{q}")

        for engname in ENGS:
            def body(eng, engname=engname):
                emit_all(P(engname, eng, sems))
            getattr(block, engname)(body)

    return nc, dbg

N_CORES = 8


import ml_dtypes
from concourse.bass_utils import run_bass_kernel_spmd


def _layout_common(inp):
    wq = np.ascontiguousarray(inp["L0_attn_w_qkv"], dtype=np.float32)
    wp = np.ascontiguousarray(inp["L0_attn_w_proj"], dtype=np.float32)
    wg = np.ascontiguousarray(inp["L0_ffn_w_gate"], dtype=np.float32)
    wu = np.ascontiguousarray(inp["L0_ffn_w_up"], dtype=np.float32)
    wd = np.ascontiguousarray(inp["L0_ffn_w_down"], dtype=np.float32)
    common = {
        "wqkvT": np.ascontiguousarray(wq.T).reshape(NCH, 128, 3 * C),
        "wprojT": np.ascontiguousarray(wp.T).reshape(NCH, 128, C),
        "wgT": np.ascontiguousarray(wg.reshape(HID // 128, 128, C).transpose(0, 2, 1)),
        "wuT": np.ascontiguousarray(wu.reshape(HID // 128, 128, C).transpose(0, 2, 1)),
        "wdT": np.ascontiguousarray(wd.T),
        "identf": np.eye(128, dtype=np.float32),
        "identb": np.eye(128, dtype=ml_dtypes.bfloat16),
        "ones": np.ones((128, 2), np.float32),
    }
    masks = np.empty((128, 4 * 512), ml_dtypes.bfloat16)
    for i in range(4):
        pp = np.arange(128)[:, None] + 128 * i
        ql = np.arange(512)[None, :]
        masks[:, 512 * i:512 * (i + 1)] = np.where(pp <= ql, 1.0, 0.0).astype(ml_dtypes.bfloat16)
    common["masks"] = masks
    consts = np.zeros((128, NCONST), np.float32)
    consts[:, CON_EPS] = 1e-5
    consts[:, CON_SCALE] = SCALE
    common["consts"] = consts
    return common


_NC_CACHE = {}


def kernel(L0_attn_x, L0_attn_w_qkv, L0_attn_w_proj, L0_ffn_w_gate, L0_ffn_w_up,
           L0_ffn_w_down):
    x = np.ascontiguousarray(np.asarray(L0_attn_x), dtype=np.float32)
    common = _layout_common(dict(
        L0_attn_w_qkv=np.asarray(L0_attn_w_qkv), L0_attn_w_proj=np.asarray(L0_attn_w_proj),
        L0_ffn_w_gate=np.asarray(L0_ffn_w_gate), L0_ffn_w_up=np.asarray(L0_ffn_w_up),
        L0_ffn_w_down=np.asarray(L0_ffn_w_down)))
    if "nc" not in _NC_CACHE:
        _NC_CACHE["nc"] = build()[0]
    nc = _NC_CACHE["nc"]
    in_maps = []
    for c in range(N_CORES):
        m = dict(common)
        m["x"] = np.ascontiguousarray(x[c])
        in_maps.append(m)
    res = run_bass_kernel_spmd(nc, in_maps, core_ids=list(range(N_CORES)))
    return np.stack([np.asarray(res.results[c]["out"]) for c in range(N_CORES)], axis=0)
